# version 08
# baseline (speedup 1.0000x reference)
import numpy as np
import concourse.bass as bass
import concourse.mybir as mybir
from concourse.bass_utils import run_bass_kernel_spmd
from contextlib import ExitStack

F32 = mybir.dt.float32
BF16 = mybir.dt.bfloat16
AF = mybir.ActivationFunctionType
ALU = mybir.AluOpType
AX = mybir.AxisListType

D = 2048
NK = 16
TOK = 1024
NT = 8
NPAIR = 16
NBLK = 4
EPS = 1e-5
IEXP = 8

ENGINES = ("tensor", "vector", "scalar", "gpsimd", "sync")


class Prog:
    def __init__(self):
        self.items = {e: [] for e in ENGINES}
        self.cnt = {}
        self.last_wait = {e: {} for e in ENGINES}

    def now(self, sem):
        return self.cnt.get(sem, 0)

    def wait(self, eng, sem, val=None):
        v = self.now(sem) if val is None else val
        if v <= 0:
            return
        if self.last_wait[eng].get(sem, -1) >= v:
            return
        self.last_wait[eng][sem] = v
        self.items[eng].append(("w", sem, v))

    def op(self, eng, fn, incs=()):
        toks = {}
        for s, a in incs:
            self.cnt[s] = self.now(s) + a
            toks[s] = self.cnt[s]
        self.items[eng].append(("o", fn, tuple(incs)))
        return toks


def build(dbg=()):
    nc = bass.Bass("TRN2", target_bir_lowering=False)

    d_x_fm = nc.dram_tensor("x_fm", [D, TOK], F32, kind="ExternalInput")
    d_x_pfm = nc.dram_tensor("x_pfm", [D, TOK], F32, kind="ExternalInput")
    d_x_tm = nc.dram_tensor("x_tm", [TOK, D], F32, kind="ExternalInput")
    d_qkg = nc.dram_tensor("qkg", [32, 128, 2048], F32, kind="ExternalInput")
    d_pkv = nc.dram_tensor("pkv", [8, 128, 8192], F32, kind="ExternalInput")
    d_vg = nc.dram_tensor("vg", [4, 128, 8192], F32, kind="ExternalInput")
    d_gateg = nc.dram_tensor("gateg", [4, 128, 8192], F32, kind="ExternalInput")
    d_outg = nc.dram_tensor("outg", [8, 128, 4096], F32, kind="ExternalInput")
    d_rgT = nc.dram_tensor("rgT", [D, IEXP], F32, kind="ExternalInput")
    d_wgg = nc.dram_tensor("wgg", [IEXP, 8, 128, 2048], F32, kind="ExternalInput")
    d_wug = nc.dram_tensor("wug", [IEXP, 8, 128, 2048], F32, kind="ExternalInput")
    d_wdg = nc.dram_tensor("wdg", [IEXP, 8, 128, 2048], F32, kind="ExternalInput")
    d_sgg = nc.dram_tensor("sgg", [16, 128, 2048], F32, kind="ExternalInput")
    d_sug = nc.dram_tensor("sug", [16, 128, 2048], F32, kind="ExternalInput")
    d_sdg = nc.dram_tensor("sdg", [8, 128, 4096], F32, kind="ExternalInput")
    d_normw = nc.dram_tensor("normw", [D], F32, kind="ExternalInput")
    d_out = nc.dram_tensor("out", [TOK, D], F32, kind="ExternalOutput")
    d_dbg = {name: nc.dram_tensor("dbg_" + name, shape, F32, kind="ExternalOutput")
             for name, shape in dbg}

    pr = Prog()

    with ExitStack() as st:
        def sbuf(name, shape, dt):
            return st.enter_context(nc.sbuf_tensor(name, shape, dt))

        A = sbuf("arenaA", [128, NK * 1024], BF16)
        B = sbuf("arenaB", [128, NK * 1024], BF16)
        C = sbuf("arenaC", [128, NK * 1024], BF16)
        Dm = sbuf("arenaD", [128, NK * 1024], BF16)
        E = sbuf("arenaE", [128, NK * 1024], BF16)
        G = sbuf("arenaG", [128, 10 * 1024], BF16)

        xn_f = A.reshape([128, NK, 1024])
        stagB = B.bitcast(F32).reshape([128, NK, 512])
        pgrp = B.reshape([128, 2, NK, 512])
        q_f = B.reshape([128, NK, 1024])
        s_pre_f = B.reshape([128, NK, 1024])
        xn_p = C.reshape([128, NK, 1024])
        k_f = C.reshape([128, NK, 1024])
        gate_sb = C.reshape([128, NT, 2048])
        h_f = C.reshape([128, NK, 1024])
        h_e = C.reshape([128, 16, 1024])
        g_sb = C.reshape([128, NT, 2048])
        xbf = Dm.reshape([128, 32, 512])
        kp_p = Dm.reshape([128, NT, 2048])
        v_tm = Dm.reshape([128, NT, 2048])
        ggrp = Dm.reshape([128, 2, NK, 512])
        y_lo = Dm.bitcast(F32).reshape([128, 4, 2048])
        v_p = E.reshape([128, NT, 2048])
        vgrp = E.reshape([128, 2, NK, 512])
        attn_sb = E.reshape([128, NT, 2048])
        y_hi = E.bitcast(F32).reshape([128, 4, 2048])
        xstage = E.bitcast(F32).reshape([128, NT, 1024])
        stagE = E.bitcast(F32).reshape([128, NK, 512])
        dd_sb = G.reshape([128, 40, 256])
        kpblk = G.reshape([128, 10, 1024])
        st_f32 = G.bitcast(F32).reshape([128, 5120])
        sqtmp = G.reshape([128, 10240])
        wgrpG = G.reshape([128, 2, 20, 256])

        def y_sl(tc, c0, c1):
            return (y_lo if tc < 4 else y_hi)[:, tc % 4, c0:c1]

        wq = sbuf("wq", [128, 2, NK, 128], BF16)
        qd = sbuf("qd", [128, 2, 32], F32)
        kd = sbuf("kd", [128, 2, 32], F32)
        sl_full = sbuf("sl_full", [128, 2, 32], F32)
        tmp_sl = sbuf("tmp_sl", [128, 2, 32], F32)
        delta_c = sbuf("delta_c", [128, 2, 256], F32)
        mask_c = sbuf("mask_c", [128, 2, 256], BF16)
        bdb = sbuf("bdb", [128, NPAIR, 64], F32)
        state = sbuf("state", [128, NPAIR, 64], F32)
        state_bf = sbuf("state_bf", [128, NPAIR, 64], BF16)
        awd = sbuf("awd", [128, 2, 2, 256], BF16)
        ident = sbuf("ident", [128, 128], BF16)
        ones_bf = sbuf("ones_bf", [128, 128], BF16)
        two_sb = sbuf("two_sb", [128, 1], F32)
        ln2 = sbuf("ln2", [128, 1], F32)
        epsb = sbuf("epsb", [128, 1], F32)
        pidx = sbuf("pidx", [128, 1], F32)
        phalf = sbuf("phalf", [128, 1], F32)
        normw_sb = sbuf("normw_sb", [128, NK], F32)
        rg_sb = sbuf("rg_sb", [128, NK, 8], BF16)
        cw = sbuf("cw", [128, NT, 8], F32)
        rr = sbuf("rr", [128, 24], F32)
        rms_acc = sbuf("rms_acc", [128, NT], F32)
        rstd_a = sbuf("rstd_a", [128, NT], F32)
        atile = sbuf("atile", [128, 1024], BF16)
        evtmp = atile.bitcast(F32).reshape([128, 512])
        out_sbv = atile.bitcast(F32).reshape([128, 2, 256])

        pb = [st.enter_context(nc.psum_tensor(f"pb{i}", [128, 512], F32)) for i in range(7)]
        pt = st.enter_context(nc.psum_tensor("pt", [128, 1024], BF16))

        semnames = ["s_wd", "s_xd", "s_t", "s_v", "s_sc", "s_g", "s_out"]
        sems = {n: st.enter_context(nc.semaphore(n)) for n in semnames}

        block = st.enter_context(nc.Block())

        bankstate = [dict(prod=None, cons=None) for _ in range(8)]

        def bank_acquire(b):
            c = bankstate[b]["cons"]
            if c:
                pr.wait("tensor", c[0], c[1])

        def mm(bank, out_ap, parts, extra_wait=()):
            bank_acquire(bank)
            for s, v in extra_wait:
                pr.wait("tensor", s, v)
            n = len(parts)
            for i, (l, r) in enumerate(parts):
                if i == n - 1:
                    tok = pr.op("tensor",
                                lambda t, l=l, r=r, s0=(i == 0): t.matmul(out_ap, l, r, start=s0, stop=True),
                                incs=[("s_t", 1)])
                    bankstate[bank]["prod"] = ("s_t", tok["s_t"])
                else:
                    pr.op("tensor",
                          lambda t, l=l, r=r, s0=(i == 0): t.matmul(out_ap, l, r, start=s0, stop=False))
            return bankstate[bank]["prod"]

        def evac(eng, bank, fn):
            p = bankstate[bank]["prod"]
            pr.wait(eng, p[0], p[1])
            s = "s_v" if eng == "vector" else "s_sc"
            tok = pr.op(eng, fn, incs=[(s, 1)])
            bankstate[bank]["cons"] = (s, tok[s])
            return (s, tok[s])

        wbuf_guard = {}

        def wdma(buf_key, fn):
            for g in wbuf_guard.get(buf_key, ()):
                pr.wait("gpsimd", g[0], g[1])
            pr.op("gpsimd", fn, incs=[("s_wd", 16)])
            return pr.now("s_wd")

        def wuse(buf_key, *toks):
            wbuf_guard[buf_key] = list(toks)

        def dump(name, src_ap, wait_sems):
            if name not in d_dbg:
                return
            for s, v in wait_sems:
                pr.wait("gpsimd", s, v)
            pr.op("gpsimd", lambda e, a=src_ap, o=d_dbg[name].ap(): e.dma_start(out=o, in_=a),
                  incs=[("s_out", 16)])

        def g_iota(out_ap, pattern, base, chan):
            pr.op("gpsimd",
                  lambda g, o=out_ap, p=pattern, b=base, c=chan:
                  g.iota(o, pattern=p, base=b, channel_multiplier=c,
                         allow_small_or_imprecise_dtypes=True),
                  incs=[("s_g", 1)])

        g_iota(pidx[:, :], [[0, 1]], 0, 1)
        g_iota(sl_full[:, :, :], [[0, 2], [1, 32]], 0, 0)
        g_iota(qd[:, :, :], [[128, 2], [0, 32]], 1, 1)
        g_iota(kd[:, :, :], [[-128, 2], [0, 32]], 255, -1)
        g_iota(delta_c[:, :, :], [[-128, 2], [1, 256]], 0, -1)
        g_iota(bdb[:, :, :], [[2, NPAIR], [0, 64]], 0, 0)
        g_iota(ident[:, :], [[-1, 128]], 0, 1)
        tok_g = pr.now("s_g")

        pr.op("vector", lambda v: v.memset(two_sb[:, :], 2.0))
        pr.op("vector", lambda v: v.memset(epsb[:, :], EPS))
        pr.op("vector", lambda v: v.memset(ones_bf[:, :], 1.0))
        pr.op("vector", lambda v: v.memset(state[:, :, :], 0.0))
        pr.wait("vector", "s_g", tok_g)
        pr.op("vector", lambda v: v.tensor_single_scalar(ident[:, :], ident[:, :], 0.0, ALU.is_equal))
        pr.op("vector", lambda v: v.tensor_single_scalar(phalf[:, :], pidx[:, :], 64.0, ALU.is_ge))
        pr.op("vector", lambda v: v.tensor_single_scalar(mask_c[:, :, :], delta_c[:, :, :], 0.0, ALU.is_ge),
              incs=[("s_v", 1)])
        tok_ones_v = pr.now("s_v")
        pr.wait("scalar", "s_v")
        pr.op("scalar", lambda s: s.activation(ln2[:, :], two_sb[:, :], AF.Ln), incs=[("s_sc", 1)])
        pr.wait("vector", "s_sc")
        pr.op("vector", lambda v: v.tensor_scalar(tmp_sl[:, :, :], sl_full[:, :, :], 1.0, -0.25, ALU.add, ALU.mult))
        pr.op("vector", lambda v: v.tensor_scalar(tmp_sl[:, :, :], tmp_sl[:, :, :], ln2[:, :], None, ALU.mult),
              incs=[("s_v", 1)])
        pr.wait("scalar", "s_v")
        pr.op("scalar", lambda s: s.activation(tmp_sl[:, :, :], tmp_sl[:, :, :], AF.Exp), incs=[("s_sc", 1)])
        pr.wait("vector", "s_sc")
        pr.op("vector", lambda v: v.scalar_tensor_tensor(sl_full[:, :, :], tmp_sl[:, :, :], 1e-5,
                                                         tmp_sl[:, :, :], ALU.mult, ALU.add),
              incs=[("s_v", 1)])
        pr.wait("vector", "s_v")
        pr.op("vector", lambda v: v.tensor_mul(qd[:, :, :], qd[:, :, :], sl_full[:, :, :]))
        pr.op("vector", lambda v: v.tensor_mul(kd[:, :, :], kd[:, :, :], sl_full[:, :, :]), incs=[("s_v", 1)])
        pr.wait("scalar", "s_v")
        pr.op("scalar", lambda s: s.activation(qd[:, :, :], qd[:, :, :], AF.Exp, scale=-1.0))
        pr.op("scalar", lambda s: s.activation(kd[:, :, :], kd[:, :, :], AF.Exp, scale=-1.0), incs=[("s_sc", 1)])
        pr.op("vector", lambda v: v.tensor_scalar(bdb[:, :, :], bdb[:, :, :], phalf[:, :], 1.0, ALU.add, ALU.add))
        pr.op("vector", lambda v: v.tensor_scalar_mul(bdb[:, :, :], bdb[:, :, :], -0.25))
        pr.op("vector", lambda v: v.tensor_scalar(bdb[:, :, :], bdb[:, :, :], ln2[:, :], None, ALU.mult),
              incs=[("s_v", 1)])
        pr.wait("scalar", "s_v")
        pr.op("scalar", lambda s: s.activation(bdb[:, :, :], bdb[:, :, :], AF.Exp), incs=[("s_sc", 1)])
        pr.wait("vector", "s_sc")
        pr.op("vector", lambda v: v.scalar_tensor_tensor(bdb[:, :, :], bdb[:, :, :], 1e-5,
                                                         bdb[:, :, :], ALU.mult, ALU.add), incs=[("s_v", 1)])
        pr.wait("vector", "s_v")
        pr.op("vector", lambda v: v.tensor_scalar_mul(bdb[:, :, :], bdb[:, :, :], -256.0), incs=[("s_v", 1)])
        pr.wait("scalar", "s_v")
        pr.op("scalar", lambda s: s.activation(bdb[:, :, :], bdb[:, :, :], AF.Exp), incs=[("s_sc", 1)])
        tok_tables_v = pr.now("s_v")
        tok_tables_sc = pr.now("s_sc")

        pr.op("sync", lambda e: e.dma_start(out=normw_sb[:, :],
                                            in_=d_normw.ap().rearrange("(k p) -> p k", p=128),
                                            allow_slow_non_contiguous=True),
              incs=[("s_xd", 16)])
        tok_normw = pr.now("s_xd")
        pr.op("gpsimd", lambda g: g.dma_start(out=rg_sb[:, :, :],
                                              in_=d_rgT.ap().rearrange("(k p) e -> p k e", p=128)),
              incs=[("s_wd", 16)])
        tok_rg = pr.now("s_wd")

        mbuf = st_f32[:, 0:512]
        msq = st_f32[:, 512:1024]
        m2b = st_f32[:, 1024:1536]
        t1b = st_f32[:, 1536:2048]

        stagbufs = [stagB, stagE]
        norm_tok = [0, 0]
        stats_tok = 0
        for grp in range(4):
            own = grp < 2
            tg = grp % 2
            src = d_x_fm if own else d_x_pfm
            dst = xn_f if own else xn_p
            sb_ = stagbufs[grp % 2]
            pr.wait("sync", "s_v", norm_tok[grp % 2])
            tok = pr.op("sync",
                        lambda e, s=src, tg=tg, sb_=sb_: e.dma_start(
                            out=sb_[:, :, :],
                            in_=s.ap().rearrange("(k p) t -> p k t", p=128)[:, :, tg * 512:(tg + 1) * 512]),
                        incs=[("s_xd", 16)])
            pr.wait("vector", "s_xd", tok["s_xd"])
            pr.wait("scalar", "s_xd", tok["s_xd"])
            pr.wait("vector", "s_t", stats_tok)
            pr.wait("scalar", "s_t", stats_tok)
            tok_cast = pr.op("vector", lambda v, sb_=sb_: v.tensor_copy(
                xbf.ap().rearrange("p a t -> p (a t)")[:, 0:NK * 512],
                sb_.ap().rearrange("p k t -> p (k t)")), incs=[("s_v", 1)])
            tok_sq = pr.op("scalar", lambda s, sb_=sb_: s.activation(
                xbf.ap().rearrange("p a t -> p (a t)")[:, NK * 512:2 * NK * 512],
                sb_.ap().rearrange("p k t -> p (k t)"), AF.Square), incs=[("s_sc", 1)])
            mm(0, pb[0][:, :], [(ones_bf[:, :], xbf[:, k, :]) for k in range(NK)],
               extra_wait=[("s_v", tok_cast["s_v"]), ("s_sc", tok_sq["s_sc"]), ("s_v", tok_ones_v)])
            stats_group = mm(1, pb[1][:, :], [(ones_bf[:, :], xbf[:, NK + k, :]) for k in range(NK)])
            stats_tok = stats_group[1]
            evac("vector", 0, lambda v: v.tensor_scalar_mul(mbuf, pb[0][:, :], 1.0 / D))
            evac("vector", 1, lambda v: v.tensor_scalar_mul(msq, pb[1][:, :], 1.0 / D))
            pr.op("vector", lambda v: v.tensor_mul(m2b, mbuf, mbuf))
            pr.op("vector", lambda v: v.tensor_sub(msq, msq, m2b), incs=[("s_v", 1)])
            pr.wait("scalar", "s_v")
            pr.op("scalar", lambda s: s.activation(m2b, msq, AF.Sqrt, bias=epsb[:, :]), incs=[("s_sc", 1)])
            pr.wait("vector", "s_sc")
            pr.op("vector", lambda v: v.reciprocal(msq, m2b))
            pr.op("vector", lambda v: v.tensor_mul(mbuf, mbuf, msq))
            for k in range(NK):
                pr.op("vector", lambda v, k=k, sb_=sb_: v.tensor_mul(t1b, sb_[:, k, :], msq))
                pr.op("vector",
                      lambda v, k=k, dst=dst, tg=tg: v.tensor_sub(
                          dst[:, k, tg * 512:(tg + 1) * 512], t1b, mbuf),
                      incs=[("s_v", 1)])
            norm_tok[grp % 2] = pr.now("s_v")
        tok_ln_all = pr.now("s_v")
        tok_xnp = tok_ln_all

        dump("xnf", xn_f.ap().rearrange("p k t -> p (k t)"), [("s_v", tok_ln_all)])
        dump("xnp", xn_p.ap().rearrange("p k t -> p (k t)"), [("s_v", tok_ln_all)])

        wuse(("pgrp", 0), ("s_v", tok_ln_all))
        wuse(("pgrp", 1), ("s_v", tok_ln_all))
        for pi, pname in enumerate(("k", "v")):
            for vg in range(4):
                gi = pi * 4 + vg
                par = gi % 2
                cols0 = D + pi * D + vg * 512
                dmatok = wdma(("pgrp", par),
                              lambda g, gi=gi, par=par: g.dma_start(
                                  out=pgrp[:, par, :, :],
                                  in_=d_pkv[gi].rearrange("p (k c) -> p k c", k=NK)))
                pr.wait("tensor", "s_wd", dmatok)
                pr.wait("tensor", "s_v", tok_ln_all)
                for tc in range(NT):
                    bank = tc % 4
                    prod = mm(bank, pb[bank][:, :],
                              [(xn_p[:, k, tc * 128:(tc + 1) * 128], pgrp[:, par, k, :]) for k in range(NK)])
                    if tc == NT - 1:
                        wuse(("pgrp", par), prod)
                    if pname == "v":
                        evac("scalar", bank,
                             lambda s, tc=tc, vg=vg, bank=bank: s.activation(
                                 v_p[:, tc, vg * 512:vg * 512 + 512], pb[bank][:, :], AF.Silu))
                    else:
                        pr.wait("scalar", "s_v", pr.now("s_v"))
                        evac("scalar", bank,
                             lambda s, bank=bank: s.activation(evtmp[:, :], pb[bank][:, :], AF.Silu))
                        pr.wait("vector", "s_sc")
                        pr.wait("vector", "s_v", tok_tables_v)
                        pr.wait("vector", "s_sc", tok_tables_sc)
                        for hh in range(8):
                            h = vg * 8 + hh
                            pr.op("vector",
                                  lambda v, tc=tc, vg=vg, hh=hh, h=h: v.tensor_scalar(
                                      kp_p[:, tc, vg * 512 + hh * 64: vg * 512 + hh * 64 + 64],
                                      evtmp[:, hh * 64:hh * 64 + 64],
                                      kd[:, tc % 2, h:h + 1], None, ALU.mult),
                                  incs=[("s_v", 1)] if hh == 7 else ())
        tok_kv_prev_sc = pr.now("s_sc")
        tok_kv_prev_v = pr.now("s_v")
        dump("kpp", kp_p.ap().rearrange("p a b -> p (a b)"), [("s_v", tok_kv_prev_v), ("s_sc", tok_kv_prev_sc)])
        dump("vpp", v_p.ap().rearrange("p a b -> p (a b)"), [("s_sc", tok_kv_prev_sc)])

        pr.wait("vector", "s_sc", tok_tables_sc)
        for blk in range(NBLK):
            for pair in range(NPAIR):
                bank = 4 + (pair % 2)
                mm(bank, pb[bank][:, 0:64],
                   [(kp_p[:, 2 * blk + cc, pair * 128:pair * 128 + 128],
                     v_p[:, 2 * blk + cc, (2 * pair) * 64:(2 * pair) * 64 + 64]) for cc in range(2)],
                   extra_wait=[("s_sc", tok_kv_prev_sc), ("s_v", tok_kv_prev_v)])
                prodB = mm(bank, pb[bank][:, 64:128],
                           [(kp_p[:, 2 * blk + cc, pair * 128:pair * 128 + 128],
                             v_p[:, 2 * blk + cc, (2 * pair + 1) * 64:(2 * pair + 1) * 64 + 64]) for cc in range(2)])
                pr.wait("vector", "s_t", prodB[1])
                pr.op("vector", lambda v, pair=pair: v.tensor_mul(
                    state[:, pair, :], state[:, pair, :], bdb[:, pair, :]))
                pr.op("vector", lambda v, pair=pair, bank=bank: v.tensor_add(
                    state[0:64, pair, :], state[0:64, pair, :], pb[bank][0:64, 0:64]))
                pr.op("vector", lambda v, pair=pair, bank=bank: v.tensor_add(
                    state[64:128, pair, :], state[64:128, pair, :], pb[bank][64:128, 64:128]))
                tok_f = pr.op("vector", lambda v, pair=pair: v.tensor_copy(
                    state_bf[:, pair, :], state[:, pair, :]), incs=[("s_v", 1)])
                bankstate[bank]["cons"] = ("s_v", tok_f["s_v"])
        tok_state = pr.now("s_v")
        tok_phaseA_t = pr.now("s_t")
        dump("state0", state.ap().rearrange("p a b -> p (a b)"), [("s_v", tok_state)])

        def gen_dd(hg):
            pr.wait("vector", "s_v", tok_tables_v)
            pr.wait("scalar", "s_sc", tok_tables_sc)
            bufs = [evtmp[:, 0:256], evtmp[:, 256:512]]
            exp_tok = [0, 0]
            mul_tok = [0, 0]
            for i in range(32):
                hh, jc = i // 2, i % 2
                h = 16 * hg + hh
                bu = i % 2
                pr.wait("vector", "s_sc", exp_tok[bu])
                pr.wait("vector", "s_v", mul_tok[bu])
                t1 = pr.op("vector", lambda v, jc=jc, h=h, bu=bu: v.tensor_scalar(
                    bufs[bu], delta_c[:, jc, :], sl_full[:, 0, h:h + 1], 0.0, ALU.mult, ALU.max),
                    incs=[("s_v", 1)])
                pr.wait("scalar", "s_v", t1["s_v"])
                t2 = pr.op("scalar", lambda s, bu=bu: s.activation(bufs[bu], bufs[bu], AF.Exp, scale=-1.0),
                           incs=[("s_sc", 1)])
                exp_tok[bu] = t2["s_sc"]
                pr.wait("vector", "s_sc", t2["s_sc"])
                t3 = pr.op("vector", lambda v, hh=hh, jc=jc, bu=bu: v.tensor_mul(
                    dd_sb[:, hh * 2 + jc, :], bufs[bu], mask_c[:, jc, :]), incs=[("s_v", 1)])
                mul_tok[bu] = t3["s_v"]
            return pr.now("s_v")

        tok_dd0 = gen_dd(0)
        tok_k_sc = 0
        for pp in range(32):
            isq = pp >= 16
            pair = pp % 16
            par = pp % 2
            cols0 = (D if not isq else 0) + pair * 128
            gidx = (16 + pair) if isq else pair
            dmatok = wdma(("wq", par),
                          lambda g, gidx=gidx, par=par: g.dma_start(
                              out=wq[:, par, :, :],
                              in_=d_qkg[gidx].rearrange("p (k c) -> p k c", k=NK)))
            pr.wait("tensor", "s_wd", dmatok)
            for tg in range(2):
                bank = (pp * 2 + tg) % 4
                dst = q_f if isq else k_f
                prod = mm(bank, pb[bank][:, :],
                          [(wq[:, par, k, :], xn_f[:, k, tg * 512:(tg + 1) * 512]) for k in range(NK)])
                if tg == 1:
                    wuse(("wq", par), prod)
                evac("scalar", bank,
                     lambda s, dst=dst, pair=pair, tg=tg, bank=bank: s.activation(
                         dst[:, pair, tg * 512:(tg + 1) * 512], pb[bank][:, :], AF.Silu))
            if pp == 15:
                tok_k_sc = pr.now("s_sc")
        tok_kq_sc = pr.now("s_sc")
        wuse(("vgrp", 0), ("s_t", tok_phaseA_t))
        wuse(("vgrp", 1), ("s_t", tok_phaseA_t))
        tok_v_sc = [0, 0, 0, 0]
        for vg in range(4):
            par = vg % 2
            cols0 = 2 * D + vg * 512
            dmatok = wdma(("vgrp", par),
                          lambda g, vg=vg, par=par: g.dma_start(
                              out=vgrp[:, par, :, :],
                              in_=d_vg[vg].rearrange("p (k c) -> p k c", k=NK)))
            pr.wait("tensor", "s_wd", dmatok)
            for tc in range(NT):
                bank = tc % 4
                prod = mm(bank, pb[bank][:, :],
                          [(xn_f[:, k, tc * 128:(tc + 1) * 128], vgrp[:, par, k, :]) for k in range(NK)])
                if tc == NT - 1:
                    wuse(("vgrp", par), prod)
                evac("scalar", bank,
                     lambda s, tc=tc, vg=vg, bank=bank: s.activation(
                         v_tm[:, tc, vg * 512:vg * 512 + 512], pb[bank][:, :], AF.Silu))
            tok_v_sc[vg] = pr.now("s_sc")
        tok_qkv_own_sc = pr.now("s_sc")


        tok_fold = tok_state
        tok_cast_last = 0
        evtmp_guard = [pr.now("s_v"), pr.now("s_v")]
        for hg in range(2):
            tok_dd = tok_dd0 if hg == 0 else gen_dd(1)
            if hg == 1:
                evtmp_guard = [pr.now("s_v"), pr.now("s_v")]
            for blk in range(NBLK):
                pr.wait("tensor", "s_sc", tok_k_sc)
                pr.wait("tensor", "s_v", tok_dd)
                for half8 in range(2):
                    c = bankstate[7]["cons"]
                    if c:
                        pr.wait("tensor", c[0], c[1])
                    for i8 in range(8):
                        idx = half8 * 8 + i8
                        lp, tc2 = idx // 2, idx % 2
                        pair = hg * 8 + lp
                        tok_tr = pr.op("tensor", lambda t, pair=pair, blk=blk, tc2=tc2, i8=i8: t.transpose(
                            pt[:, i8 * 128:i8 * 128 + 128],
                            k_f[:, pair, blk * 256 + tc2 * 128: blk * 256 + tc2 * 128 + 128],
                            ident[:, :]), incs=[("s_t", 1)] if i8 == 7 else ())
                    bankstate[7]["prod"] = ("s_t", pr.now("s_t"))
                    pr.wait("scalar", "s_t", pr.now("s_t"))
                    for i8 in range(8):
                        idx = half8 * 8 + i8
                        lp, tc2 = idx // 2, idx % 2
                        for hb in range(2):
                            h = 2 * (hg * 8 + lp) + hb
                            tok_e = pr.op("scalar", lambda s, i8=i8, tc2=tc2, lp=lp, hb=hb, h=h: s.activation(
                                kpblk[:, 8 + tc2, lp * 128 + hb * 64: lp * 128 + hb * 64 + 64],
                                pt[:, i8 * 128 + hb * 64: i8 * 128 + hb * 64 + 64],
                                AF.Copy, scale=kd[:, tc2, h:h + 1]),
                                incs=[("s_sc", 1)] if i8 == 7 and hb == 1 else ())
                    bankstate[7]["cons"] = ("s_sc", tok_e["s_sc"])
                tok_kp = ("s_sc", pr.now("s_sc"))

                for lp in range(8):
                    pair = hg * 8 + lp
                    pr.wait("tensor", "s_sc", tok_kq_sc)
                    pr.wait("tensor", "s_sc", tok_v_sc[pair // 4])
                    bawA = 4 if lp % 2 == 0 else 0
                    bawB = 5 if lp % 2 == 0 else 1
                    bii = 6 if lp % 2 == 0 else 3
                    for jc in range(2):
                        mm(bawA, pb[bawA][:, jc * 256:(jc + 1) * 256],
                           [(k_f[0:64, pair, blk * 256 + jc * 128: blk * 256 + jc * 128 + 128],
                             q_f[0:64, pair, blk * 256: blk * 256 + 256])])
                        mm(bawB, pb[bawB][:, jc * 256:(jc + 1) * 256],
                           [(k_f[64:128, pair, blk * 256 + jc * 128: blk * 256 + jc * 128 + 128],
                             q_f[64:128, pair, blk * 256: blk * 256 + 256])])
                    pp_ = lp % 2
                    for hb in range(2):
                        h = 2 * pair + hb
                        hh = h - 16 * hg
                        baw = bawA if hb == 0 else bawB
                        if hb == 1:
                            pr.wait("vector", "s_t", pr.now("s_t"))
                        evac("vector", baw,
                             lambda v, baw=baw, hh=hh, pp_=pp_: v.tensor_mul(
                                 awd[:, pp_, :, :], pb[baw][:, 0:512].rearrange("p (j c) -> p j c", j=2),
                                 dd_sb[:, hh * 2:hh * 2 + 2, :]))
                        tok_awd = pr.now("s_v")
                        for ic in range(2):
                            off = hb * 256 + ic * 128
                            mm(bii, pb[bii][:, off:off + 64],
                               [(awd[:, pp_, jc, ic * 128:ic * 128 + 128],
                                 v_tm[:, 2 * blk + jc, h * 64:h * 64 + 64]) for jc in range(2)],
                               extra_wait=[("s_v", tok_awd)])
                            pm = 0 if hb == 0 else 64
                            mm(bii, pb[bii][:, off + 64:off + 128],
                               [(q_f[pm:pm + 64, pair, blk * 256 + ic * 128: blk * 256 + ic * 128 + 128],
                                 state_bf[pm:pm + 64, pair, :])],
                               extra_wait=[tok_fold if isinstance(tok_fold, tuple) else ("s_v", tok_fold)])
                    prod = bankstate[bii]["prod"]
                    pr.wait("scalar", prod[0], prod[1])
                    pr.wait("scalar", "s_v", evtmp_guard[pp_])
                    for hb in range(2):
                        for ic in range(2):
                            off = hb * 256 + ic * 128
                            eoff = pp_ * 256 + hb * 128 + ic * 64
                            pr.op("scalar", lambda s, bii=bii, off=off, eoff=eoff, ic=ic, hb=hb, pair=pair: s.activation(
                                evtmp[:, eoff:eoff + 64], pb[bii][:, off + 64:off + 128], AF.Copy,
                                scale=qd[:, ic, 2 * pair + hb: 2 * pair + hb + 1]),
                                incs=[("s_sc", 1)] if hb == 1 and ic == 1 else ())
                    tok_sc = pr.now("s_sc")
                    pr.wait("vector", "s_sc", tok_sc)
                    out4 = attn_sb.ap()[:, 2 * blk:2 * blk + 2, 2 * pair * 64:(2 * pair + 2) * 64] \
                        .rearrange("p i (hb c) -> p hb i c", hb=2)
                    in04 = evtmp[:, pp_ * 256:pp_ * 256 + 256].rearrange("p (hb i c) -> p hb i c", hb=2, i=2)
                    in14 = pb[bii][:, 0:512].rearrange("p (hb i c2) -> p hb i c2", hb=2, i=2)[:, :, :, 0:64]
                    tok_a = pr.op("vector", lambda v, out4=out4, in04=in04, in14=in14: v.tensor_add(
                        out4, in04, in14), incs=[("s_v", 1)])
                    bankstate[bii]["cons"] = ("s_v", tok_a["s_v"])
                    evtmp_guard[pp_] = tok_a["s_v"]
                    bst = 2
                    mm(bst, pb[bst][:, 0:64],
                       [(kpblk[:, 8 + cc, lp * 128:lp * 128 + 128],
                         v_tm[:, 2 * blk + cc, (2 * pair) * 64:(2 * pair) * 64 + 64]) for cc in range(2)],
                       extra_wait=[tok_kp])
                    prodB = mm(bst, pb[bst][:, 64:128],
                               [(kpblk[:, 8 + cc, lp * 128:lp * 128 + 128],
                                 v_tm[:, 2 * blk + cc, (2 * pair + 1) * 64:(2 * pair + 1) * 64 + 64]) for cc in range(2)])
                    pr.wait("vector", "s_t", prodB[1])
                    pr.wait("vector", "s_sc", tok_cast_last)
                    pr.op("vector", lambda v, pair=pair: v.tensor_mul(
                        state[:, pair, :], state[:, pair, :], bdb[:, pair, :]))
                    pr.op("vector", lambda v, pair=pair: v.tensor_add(
                        state[0:64, pair, :], state[0:64, pair, :], pb[2][0:64, 0:64]))
                    tok_fv = pr.op("vector", lambda v, pair=pair: v.tensor_add(
                        state[64:128, pair, :], state[64:128, pair, :], pb[2][64:128, 64:128]),
                        incs=[("s_v", 1)])
                    pr.wait("scalar", "s_v", tok_fv["s_v"])
                    tok_f = pr.op("scalar", lambda s, pair=pair: s.copy(
                        state_bf[:, pair, :], state[:, pair, :]), incs=[("s_sc", 1)])
                    bankstate[2]["cons"] = ("s_v", tok_fv["s_v"])
                    tok_cast_last = tok_f["s_sc"]
                    tok_fold = ("s_sc", tok_f["s_sc"])
        tok_attn = pr.now("s_v")
        tok_scan_t = pr.now("s_t")
        dump("attn", attn_sb.ap().rearrange("p a b -> p (a b)"), [("s_v", tok_attn)])

        pr.wait("scalar", "s_v", tok_attn)
        for tc in range(NT):
            pr.op("scalar", lambda s, tc=tc: s.activation(
                sqtmp[:, 0:2048], attn_sb[:, tc, :], AF.Square, accum_out=rms_acc[:, tc:tc + 1]),
                incs=[("s_sc", 1)] if tc == NT - 1 else ())
        pr.wait("vector", "s_sc")
        pr.op("vector", lambda v: v.tensor_scalar_mul(rms_acc[:, :], rms_acc[:, :], 1.0 / D), incs=[("s_v", 1)])
        pr.wait("scalar", "s_v")
        pr.op("scalar", lambda s: s.activation(rstd_a[:, :], rms_acc[:, :], AF.Sqrt, bias=epsb[:, :]), incs=[("s_sc", 1)])
        pr.wait("vector", "s_sc")
        pr.op("vector", lambda v: v.reciprocal(rstd_a[:, :], rstd_a[:, :]), incs=[("s_v", 1)])
        tok_rstd = pr.now("s_v")

        pr.wait("tensor", "s_wd", tok_rg)
        wuse(("ggrp", 0), ("s_t", tok_scan_t))
        wuse(("ggrp", 1), ("s_t", tok_scan_t))
        for gg in range(4):
            par = gg % 2
            dmatok = wdma(("ggrp", par),
                          lambda g, gg=gg, par=par: g.dma_start(
                              out=ggrp[:, par, :, :],
                              in_=d_gateg[gg].rearrange("p (k c) -> p k c", k=NK)))
            pr.wait("tensor", "s_wd", dmatok)
            for tc in range(NT):
                bank = tc % 2
                bank_acquire(bank)
                if gg == 0:
                    bank_acquire(4)
                for k in range(NK):
                    isl = k == NK - 1
                    tokg = pr.op("tensor", lambda t, k=k, tc=tc, par=par, bank=bank: t.matmul(
                        pb[bank][:, :], xn_f[:, k, tc * 128:(tc + 1) * 128], ggrp[:, par, k, :],
                        start=(k == 0), stop=isl), incs=[("s_t", 1)] if isl else ())
                    if gg == 0:
                        tokr = pr.op("tensor", lambda t, k=k, tc=tc: t.matmul(
                            pb[4][:, tc * 8:tc * 8 + 8], xn_f[:, k, tc * 128:(tc + 1) * 128],
                            rg_sb[:, k, :], start=(k == 0), stop=isl),
                            incs=[("s_t", 1)] if isl else ())
                bankstate[bank]["prod"] = ("s_t", tokg["s_t"])
                if gg == 0:
                    bankstate[4]["prod"] = ("s_t", tokr["s_t"])
                if tc == NT - 1:
                    wuse(("ggrp", par), bankstate[bank]["prod"])
                evac("scalar", bank, lambda s, tc=tc, gg=gg, bank=bank: s.activation(
                    gate_sb[:, tc, gg * 512:(gg + 1) * 512], pb[bank][:, :], AF.Sigmoid))
        tok_gate_sc = pr.now("s_sc")

        prod4 = bankstate[4]["prod"]
        pr.wait("vector", prod4[0], prod4[1])
        def vb(fn):
            pr.op("vector", fn, incs=[("s_v", 1)])
            pr.wait("vector", "s_v")
        for tc in range(NT):
            zs = pb[4][:, tc * 8:tc * 8 + 8]
            vb(lambda v, zs=zs: v.reduce_max(rr[:, 0:1], zs, AX.X))
            vb(lambda v, zs=zs: v.tensor_single_scalar(rr[:, 16:24], zs, rr[:, 0:1], ALU.is_equal))
            vb(lambda v, zs=zs: v.scalar_tensor_tensor(rr[:, 8:16], rr[:, 16:24], -1e30, zs,
                                                       ALU.mult, ALU.add))
            vb(lambda v: v.reduce_max(rr[:, 1:2], rr[:, 8:16], AX.X))
            vb(lambda v, zs=zs: v.tensor_single_scalar(rr[:, 16:24], zs, rr[:, 1:2], ALU.is_ge))
            vb(lambda v: v.tensor_scalar_mul(rr[:, 0:1], rr[:, 0:1], -1.0))
            pr.wait("scalar", "s_v")
            pr.op("scalar", lambda s, zs=zs: s.activation(rr[:, 2:10], zs, AF.Exp, bias=rr[:, 0:1]),
                  incs=[("s_sc", 1)])
            pr.wait("vector", "s_sc")
            vb(lambda v: v.tensor_mul(rr[:, 2:10], rr[:, 2:10], rr[:, 16:24]))
            vb(lambda v: v.reduce_sum(rr[:, 1:2], rr[:, 2:10], AX.X))
            vb(lambda v: v.reciprocal(rr[:, 1:2], rr[:, 1:2]))
            tok_cw = pr.op("vector", lambda v, tc=tc: v.tensor_scalar(
                cw[:, tc, :], rr[:, 2:10], rr[:, 1:2], None, ALU.mult), incs=[("s_v", 1)])
        bankstate[4]["cons"] = ("s_v", tok_cw["s_v"])
        tok_cw_all = pr.now("s_v")
        dump("cw", cw.ap().rearrange("p a b -> p (a b)"), [("s_v", tok_cw_all)])

        s1tmp = st_f32[:, 1024:3072]
        pr.wait("vector", "s_sc", tok_gate_sc)
        pr.wait("vector", "s_v", tok_rstd)
        for tc in range(NT):
            pr.op("vector", lambda v, tc=tc: v.tensor_scalar(
                s1tmp, attn_sb[:, tc, :], rstd_a[:, tc:tc + 1], None, ALU.mult))
            pr.op("vector", lambda v, tc=tc: v.tensor_mul(attn_sb[:, tc, :], s1tmp, gate_sb[:, tc, :]),
                  incs=[("s_v", 1)])
        tok_spre_tm = pr.now("s_v")

        pr.wait("tensor", "s_v", tok_spre_tm)
        pr.wait("vector", "s_xd", tok_normw)
        for bt in range(NT * NK // 8):
            c = bankstate[7]["cons"]
            if c:
                pr.wait("tensor", c[0], c[1])
            for i8 in range(8):
                idx = bt * 8 + i8
                tc, k = idx // NK, idx % NK
                pr.op("tensor", lambda t, tc=tc, k=k, i8=i8: t.transpose(
                    pt[:, i8 * 128:i8 * 128 + 128],
                    attn_sb[:, tc, k * 128:(k + 1) * 128], ident[:, :]),
                    incs=[("s_t", 1)] if i8 == 7 else ())
            bankstate[7]["prod"] = ("s_t", pr.now("s_t"))
            pr.wait("vector", "s_t", pr.now("s_t"))
            for i8 in range(8):
                idx = bt * 8 + i8
                tc, k = idx // NK, idx % NK
                tok_e = pr.op("vector", lambda v, tc=tc, k=k, i8=i8: v.tensor_scalar(
                    s_pre_f[:, k, tc * 128:(tc + 1) * 128], pt[:, i8 * 128:i8 * 128 + 128],
                    normw_sb[:, k:k + 1], None, ALU.mult),
                    incs=[("s_v", 1)] if i8 == 7 else ())
            bankstate[7]["cons"] = ("s_v", tok_e["s_v"])
        tok_spref = pr.now("s_v")
        dump("spref", s_pre_f.ap().rearrange("p k t -> p (k t)"), [("s_v", tok_spref)])

        atile_guard = None
        for m in range(NK):
            for wi, wsrc in ((0, d_sgg), (1, d_sug)):
                par = (m * 2 + wi) % 2
                dmatok = wdma(("wq", par),
                              lambda g, wsrc=wsrc, m=m, par=par: g.dma_start(
                                  out=wq[:, par, :, :],
                                  in_=wsrc[m].rearrange("p (k c) -> p k c", k=NK)))
                pr.wait("tensor", "s_wd", dmatok)
                for tg in range(2):
                    bank = (m * 4 + wi * 2 + tg) % 4
                    prod = mm(bank, pb[bank][:, :],
                              [(wq[:, par, k, :], xn_f[:, k, tg * 512:(tg + 1) * 512]) for k in range(NK)])
                    if tg == 1:
                        wuse(("wq", par), prod)
                    if wi == 0:
                        if atile_guard:
                            pr.wait("scalar", atile_guard[0], atile_guard[1])
                        evac("scalar", bank, lambda s, tg=tg, bank=bank: s.activation(
                            atile[:, tg * 512:(tg + 1) * 512], pb[bank][:, :], AF.Silu))
                    else:
                        pr.wait("vector", "s_sc")
                        tok_h = evac("vector", bank, lambda v, m=m, tg=tg, bank=bank: v.tensor_mul(
                            h_f[:, m, tg * 512:(tg + 1) * 512], pb[bank][:, :],
                            atile[:, tg * 512:(tg + 1) * 512]))
                        if tg == 1:
                            atile_guard = tok_h
        tok_hf = pr.now("s_v")
        wuse(("wgrpG", 0), ("s_v", tok_spre_tm), ("s_sc", tok_gate_sc), ("s_t", tok_scan_t))
        wuse(("wgrpG", 1), ("s_v", tok_spre_tm), ("s_sc", tok_gate_sc), ("s_t", tok_scan_t))
        for og in range(8):
            par = og % 2
            dmatok = wdma(("wgrpG", par),
                          lambda g, og=og, par=par: g.dma_start(
                              out=wgrpG[:, par, 0:16, :],
                              in_=d_sdg[og].rearrange("p (k c) -> p k c", k=NK)))
            pr.wait("tensor", "s_wd", dmatok)
            pr.wait("tensor", "s_v", tok_hf)
            for tc in range(NT):
                bank = tc % 4
                prod = mm(bank, pb[bank][:, 0:256],
                          [(h_f[:, k, tc * 128:(tc + 1) * 128], wgrpG[:, par, k, :]) for k in range(NK)])
                if tc == NT - 1:
                    wuse(("wgrpG", par), prod)
                evac("vector", bank, lambda v, tc=tc, og=og, bank=bank: v.tensor_copy(
                    y_sl(tc, og * 256, og * 256 + 256), pb[bank][:, 0:256]))
        tok_y_shared = pr.now("s_v")

        for e in range(IEXP):
            for m in range(8):
                for wi, wsrc in ((0, d_wgg), (1, d_wug)):
                    par = (m * 2 + wi) % 2
                    dmatok = wdma(("wq", par),
                                  lambda g, wsrc=wsrc, e=e, m=m, par=par: g.dma_start(
                                      out=wq[:, par, :, :],
                                      in_=wsrc[e, m].rearrange("p (k c) -> p k c", k=NK)))
                    pr.wait("tensor", "s_wd", dmatok)
                    for tg in range(2):
                        bank = (m * 4 + wi * 2 + tg) % 4
                        prod = mm(bank, pb[bank][:, :],
                                  [(wq[:, par, k, :], xn_f[:, k, tg * 512:(tg + 1) * 512]) for k in range(NK)])
                        if tg == 1:
                            wuse(("wq", par), prod)
                        if wi == 0:
                            if atile_guard:
                                pr.wait("scalar", atile_guard[0], atile_guard[1])
                            evac("scalar", bank, lambda s, tg=tg, bank=bank: s.activation(
                                atile[:, tg * 512:(tg + 1) * 512], pb[bank][:, :], AF.Silu))
                        else:
                            pr.wait("vector", "s_sc")
                            tok_h = evac("vector", bank, lambda v, m=m, tg=tg, bank=bank: v.tensor_mul(
                                h_e[:, m, tg * 512:(tg + 1) * 512], pb[bank][:, :],
                                atile[:, tg * 512:(tg + 1) * 512]))
                            if tg == 1:
                                atile_guard = tok_h
            tok_he = pr.now("s_v")
            for og in range(8):
                par = og % 2
                dmatok = wdma(("wgrpG", par),
                              lambda g, e=e, og=og, par=par: g.dma_start(
                                  out=wgrpG[:, par, 0:8, :],
                                  in_=d_wdg[e, og].rearrange("p (k c) -> p k c", k=8)))
                pr.wait("tensor", "s_wd", dmatok)
                pr.wait("tensor", "s_v", tok_he)
                for tc in range(NT):
                    bank = tc % 4
                    prod = mm(bank, pb[bank][:, 0:256],
                              [(h_e[:, k, tc * 128:(tc + 1) * 128], wgrpG[:, par, k, :]) for k in range(8)])
                    if tc == NT - 1:
                        wuse(("wgrpG", par), prod)
                    pr.wait("vector", "s_v", tok_cw_all)
                    pr.wait("vector", "s_v", tok_y_shared)
                    evac("vector", bank, lambda v, tc=tc, og=og, e=e, bank=bank: v.scalar_tensor_tensor(
                        y_sl(tc, og * 256, og * 256 + 256), pb[bank][:, 0:256], cw[:, tc, e:e + 1],
                        y_sl(tc, og * 256, og * 256 + 256), ALU.mult, ALU.add))
        tok_y = pr.now("s_v")
        dump("y", y_lo.ap().rearrange("p a b -> p (a b)"), [("s_v", tok_y)])

        pr.wait("scalar", "s_v", tok_y)
        tok_g_tc = [0] * NT
        for tc in list(range(4, NT)) + list(range(4)):
            t_ = pr.op("scalar", lambda s, tc=tc: s.activation(
                g_sb[:, tc, :], (y_lo if tc < 4 else y_hi)[:, tc % 4, :], AF.Silu),
                incs=[("s_sc", 1)])
            tok_g_tc[tc] = t_["s_sc"]
            if tc == NT - 1:
                tok_yhi_sc = t_["s_sc"]
        tok_g_sc = pr.now("s_sc")
        dump("g", g_sb.ap().rearrange("p a b -> p (a b)"), [("s_sc", tok_g_sc)])

        n_out = 0
        sout_base = pr.now("s_out")
        for meta in range(2):
            pr.wait("sync", "s_sc", tok_yhi_sc)
            tok_x = pr.op("sync", lambda s, meta=meta: s.dma_start(
                out=xstage[:, :, :],
                in_=d_x_tm.ap().rearrange("(a p) c -> p a c", p=128)[:, :, meta * 1024:(meta + 1) * 1024]),
                incs=[("s_xd", 16)])
            for ogl in range(4):
                og = meta * 4 + ogl
                par = og % 2
                dmatok = wdma(("wgrpG", par),
                              lambda g, og=og, par=par: g.dma_start(
                                  out=wgrpG[:, par, 0:16, :],
                                  in_=d_outg[og].rearrange("p (k c) -> p k c", k=NK)))
                pr.wait("tensor", "s_wd", dmatok)
                pr.wait("tensor", "s_v", tok_spref)
                for tc in range(NT):
                    bank = tc % 4
                    prod = mm(bank, pb[bank][:, 0:256],
                              [(s_pre_f[:, k, tc * 128:(tc + 1) * 128], wgrpG[:, par, k, :]) for k in range(NK)])
                    if tc == NT - 1:
                        wuse(("wgrpG", par), prod)
                    pr.wait("vector", "s_xd", tok_x["s_xd"])
                    pr.wait("vector", "s_sc", tok_g_tc[tc])
                    obuf = n_out % 2
                    pr.wait("vector", "s_out", sout_base + max(0, (n_out - 1) * 16))
                    evac("vector", bank, lambda v, tc=tc, og=og, bank=bank: v.tensor_mul(
                        y_lo[:, 0, 0:256], pb[bank][:, 0:256], g_sb[:, tc, og * 256:(og + 1) * 256]))
                    tok_o = pr.op("vector", lambda v, tc=tc, ogl=ogl, obuf=obuf: v.tensor_add(
                        out_sbv[:, obuf, :], y_lo[:, 0, 0:256],
                        xstage[:, tc, ogl * 256:(ogl + 1) * 256]), incs=[("s_v", 1)])
                    pr.wait("sync", "s_v", tok_o["s_v"])
                    pr.op("sync", lambda s, tc=tc, og=og, obuf=obuf: s.dma_start(
                        out=d_out.ap()[tc * 128:(tc + 1) * 128, og * 256:(og + 1) * 256],
                        in_=out_sbv[:, obuf, :]), incs=[("s_out", 16)])
                    n_out += 1
        pr.wait("sync", "s_out", pr.now("s_out"))

        def run_engine(eng_name):
            def f(eng):
                for item in pr.items[eng_name]:
                    if item[0] == "w":
                        eng.wait_ge(sems[item[1]], item[2])
                    else:
                        inst = item[1](eng)
                        for s, a in item[2]:
                            inst.then_inc(sems[s], a)
            return f

        block.tensor(run_engine("tensor"))
        block.vector(run_engine("vector"))
        block.scalar(run_engine("scalar"))
        block.gpsimd(run_engine("gpsimd"))
        block.sync(run_engine("sync"))

    return nc


def _prep_inputs(L0_attn_hidden_states, L0_attn_qkv_w, L0_attn_out_w, L0_attn_gate_w,
                 L0_attn_norm_w, L0_ffn_gate_w, L0_ffn_wg, L0_ffn_wu, L0_ffn_wd,
                 L0_ffn_sg, L0_ffn_su, L0_ffn_sd):
    x = np.asarray(L0_attn_hidden_states)
    qkv = np.asarray(L0_attn_qkv_w)
    qkv_re = qkv.reshape(32, 3, 64, 2048).transpose(1, 0, 2, 3).reshape(6144, 2048)
    qkvT = np.ascontiguousarray(qkv_re.T)

    def grp(W, w):
        C = W.shape[1]
        return np.ascontiguousarray(
            W.reshape(16, 128, C).transpose(1, 0, 2)
             .reshape(128, 16, C // w, w).transpose(2, 0, 1, 3)
             .reshape(C // w, 128, 16 * w))

    gateT = np.asarray(L0_attn_gate_w).T
    outT = np.asarray(L0_attn_out_w).T
    sgw = np.asarray(L0_ffn_sg); suw = np.asarray(L0_ffn_su); sdw = np.asarray(L0_ffn_sd)
    wgw = np.asarray(L0_ffn_wg); wuw = np.asarray(L0_ffn_wu); wdw = np.asarray(L0_ffn_wd)
    qk_all = grp(qkvT[:, 0:4096], 128)
    qkg = np.concatenate([qk_all[16:32], qk_all[0:16]], axis=0)
    shared = dict(
        qkg=np.ascontiguousarray(qkg),
        pkv=grp(qkvT[:, 2048:6144], 512),
        vg=grp(qkvT[:, 4096:6144], 512),
        gateg=grp(gateT, 512),
        outg=grp(outT, 256),
        rgT=np.ascontiguousarray(np.asarray(L0_ffn_gate_w).T),
        wgg=np.stack([grp(wgw[e], 128) for e in range(8)]),
        wug=np.stack([grp(wuw[e], 128) for e in range(8)]),
        wdg=np.stack([np.ascontiguousarray(
            sdwe.reshape(8, 128, 2048).transpose(1, 0, 2)
                .reshape(128, 8, 8, 256).transpose(2, 0, 1, 3).reshape(8, 128, 2048))
            for sdwe in wdw]),
        sgg=grp(sgw, 128),
        sug=grp(suw, 128),
        sdg=grp(sdw, 256),
        normw=np.ascontiguousarray(L0_attn_norm_w),
    )
    zeros_fm = np.zeros((D, TOK), np.float32)
    in_maps = []
    for c in range(8):
        b, h = c // 2, c % 2
        own = x[b, h * TOK:(h + 1) * TOK, :]
        m = dict(shared)
        m["x_fm"] = np.ascontiguousarray(own.T)
        m["x_tm"] = np.ascontiguousarray(own)
        m["x_pfm"] = np.ascontiguousarray(x[b, 0:TOK, :].T) if h == 1 else zeros_fm
        in_maps.append(m)
    return in_maps


def kernel(**inputs):
    in_maps = _prep_inputs(**inputs)
    nc = build()
    res = run_bass_kernel_spmd(nc, in_maps, core_ids=list(range(8)))
    out = np.empty((4, 2048, 2048), np.float32)
    for c in range(8):
        b, h = c // 2, c % 2
        out[b, h * TOK:(h + 1) * TOK, :] = res.results[c]["out"]
    return out


# version 10
# speedup vs baseline: 1.0271x; 1.0271x over previous
import numpy as np
import concourse.bass as bass
import concourse.mybir as mybir
from concourse.bass_utils import run_bass_kernel_spmd
from contextlib import ExitStack

F32 = mybir.dt.float32
BF16 = mybir.dt.bfloat16
AF = mybir.ActivationFunctionType
ALU = mybir.AluOpType
AX = mybir.AxisListType

D = 2048
NK = 16
TOK = 1024
NT = 8
NPAIR = 16
NBLK = 4
EPS = 1e-5
IEXP = 8

ENGINES = ("tensor", "vector", "scalar", "gpsimd", "sync")


class Prog:
    def __init__(self):
        self.items = {e: [] for e in ENGINES}
        self.cnt = {}
        self.last_wait = {e: {} for e in ENGINES}

    def now(self, sem):
        return self.cnt.get(sem, 0)

    def wait(self, eng, sem, val=None):
        v = self.now(sem) if val is None else val
        if v <= 0:
            return
        if self.last_wait[eng].get(sem, -1) >= v:
            return
        self.last_wait[eng][sem] = v
        self.items[eng].append(("w", sem, v))

    def op(self, eng, fn, incs=()):
        toks = {}
        for s, a in incs:
            self.cnt[s] = self.now(s) + a
            toks[s] = self.cnt[s]
        self.items[eng].append(("o", fn, tuple(incs)))
        return toks


def build(dbg=()):
    nc = bass.Bass("TRN2", target_bir_lowering=False)

    d_x_fm = nc.dram_tensor("x_fm", [D, TOK], F32, kind="ExternalInput")
    d_x_pfm = nc.dram_tensor("x_pfm", [D, TOK], F32, kind="ExternalInput")
    d_x_tm = nc.dram_tensor("x_tm", [TOK, D], F32, kind="ExternalInput")
    d_qkg = nc.dram_tensor("qkg", [32, 128, 2048], F32, kind="ExternalInput")
    d_pkv = nc.dram_tensor("pkv", [8, 128, 8192], F32, kind="ExternalInput")
    d_vg = nc.dram_tensor("vg", [4, 128, 8192], F32, kind="ExternalInput")
    d_gateg = nc.dram_tensor("gateg", [4, 128, 8192], F32, kind="ExternalInput")
    d_outg = nc.dram_tensor("outg", [8, 128, 4096], F32, kind="ExternalInput")
    d_rgT = nc.dram_tensor("rgT", [D, IEXP], F32, kind="ExternalInput")
    d_wgg = nc.dram_tensor("wgg", [IEXP, 8, 128, 2048], F32, kind="ExternalInput")
    d_wug = nc.dram_tensor("wug", [IEXP, 8, 128, 2048], F32, kind="ExternalInput")
    d_wdg = nc.dram_tensor("wdg", [IEXP, 8, 128, 2048], F32, kind="ExternalInput")
    d_sgg = nc.dram_tensor("sgg", [16, 128, 2048], F32, kind="ExternalInput")
    d_sug = nc.dram_tensor("sug", [16, 128, 2048], F32, kind="ExternalInput")
    d_sdg = nc.dram_tensor("sdg", [8, 128, 4096], F32, kind="ExternalInput")
    d_normw = nc.dram_tensor("normw", [D], F32, kind="ExternalInput")
    d_out = nc.dram_tensor("out", [TOK, D], F32, kind="ExternalOutput")
    d_dbg = {name: nc.dram_tensor("dbg_" + name, shape, F32, kind="ExternalOutput")
             for name, shape in dbg}

    pr = Prog()

    with ExitStack() as st:
        def sbuf(name, shape, dt):
            return st.enter_context(nc.sbuf_tensor(name, shape, dt))

        A = sbuf("arenaA", [128, NK * 1024], BF16)
        B = sbuf("arenaB", [128, NK * 1024], BF16)
        C = sbuf("arenaC", [128, NK * 1024], BF16)
        Dm = sbuf("arenaD", [128, NK * 1024], BF16)
        E = sbuf("arenaE", [128, NK * 1024], BF16)
        G = sbuf("arenaG", [128, 10 * 1024], BF16)

        xn_f = A.reshape([128, NK, 1024])
        stagB = B.bitcast(F32).reshape([128, NK, 512])
        pgrp = B.reshape([128, 2, NK, 512])
        q_f = B.reshape([128, NK, 1024])
        s_pre_f = B.reshape([128, NK, 1024])
        xn_p = C.reshape([128, NK, 1024])
        k_f = C.reshape([128, NK, 1024])
        gate_sb = C.reshape([128, NT, 2048])
        h_f = C.reshape([128, NK, 1024])
        h_e = C.reshape([128, 16, 1024])
        g_sb = C.reshape([128, NT, 2048])
        xbf = Dm.reshape([128, 32, 512])
        kp_p = Dm.reshape([128, NT, 2048])
        v_tm = Dm.reshape([128, NT, 2048])
        ggrp = Dm.reshape([128, 2, NK, 512])
        y_lo = Dm.bitcast(F32).reshape([128, 4, 2048])
        v_p = E.reshape([128, NT, 2048])
        vgrp = E.reshape([128, 2, NK, 512])
        attn_sb = E.reshape([128, NT, 2048])
        y_hi = E.bitcast(F32).reshape([128, 4, 2048])
        xstage = E.bitcast(F32).reshape([128, NT, 1024])
        stagE = E.bitcast(F32).reshape([128, NK, 512])
        dd_sb = G.reshape([128, 40, 256])
        kpblk = G.reshape([128, 10, 1024])
        st_f32 = G.bitcast(F32).reshape([128, 5120])
        sqtmp = G.reshape([128, 10240])
        wgrpG = G.reshape([128, 2, 20, 256])

        def y_sl(tc, c0, c1):
            return (y_lo if tc < 4 else y_hi)[:, tc % 4, c0:c1]

        wq = sbuf("wq", [128, 2, NK, 128], BF16)
        qd = sbuf("qd", [128, 2, 32], F32)
        kd = sbuf("kd", [128, 2, 32], F32)
        sl_full = sbuf("sl_full", [128, 2, 32], F32)
        tmp_sl = sbuf("tmp_sl", [128, 2, 32], F32)
        delta_c = sbuf("delta_c", [128, 2, 256], F32)
        mask_c = sbuf("mask_c", [128, 2, 256], BF16)
        bdb = sbuf("bdb", [128, NPAIR, 64], F32)
        state = sbuf("state", [128, NPAIR, 64], F32)
        state_bf = sbuf("state_bf", [128, NPAIR, 64], BF16)
        awd = sbuf("awd", [128, 2, 2, 256], BF16)
        ident = sbuf("ident", [128, 128], BF16)
        ones_bf = sbuf("ones_bf", [128, 128], BF16)
        two_sb = sbuf("two_sb", [128, 1], F32)
        ln2 = sbuf("ln2", [128, 1], F32)
        epsb = sbuf("epsb", [128, 1], F32)
        pidx = sbuf("pidx", [128, 1], F32)
        phalf = sbuf("phalf", [128, 1], F32)
        normw_sb = sbuf("normw_sb", [128, NK], F32)
        rg_sb = sbuf("rg_sb", [128, NK, 8], BF16)
        cw = sbuf("cw", [128, NT, 8], F32)
        rr = sbuf("rr", [128, 24], F32)
        rms_acc = sbuf("rms_acc", [128, NT], F32)
        rstd_a = sbuf("rstd_a", [128, NT], F32)
        atile = sbuf("atile", [128, 1024], BF16)
        evtmp = atile.bitcast(F32).reshape([128, 512])
        out_sbv = atile.bitcast(F32).reshape([128, 2, 256])

        pb = [st.enter_context(nc.psum_tensor(f"pb{i}", [128, 512], F32)) for i in range(7)]
        pt = st.enter_context(nc.psum_tensor("pt", [128, 1024], BF16))

        semnames = ["s_wd", "s_xd", "s_t", "s_v", "s_sc", "s_g", "s_out"]
        sems = {n: st.enter_context(nc.semaphore(n)) for n in semnames}

        block = st.enter_context(nc.Block())

        bankstate = [dict(prod=None, cons=None) for _ in range(8)]

        def bank_acquire(b):
            c = bankstate[b]["cons"]
            if c:
                pr.wait("tensor", c[0], c[1])

        def mm(bank, out_ap, parts, extra_wait=()):
            bank_acquire(bank)
            for s, v in extra_wait:
                pr.wait("tensor", s, v)
            n = len(parts)
            for i, (l, r) in enumerate(parts):
                if i == n - 1:
                    tok = pr.op("tensor",
                                lambda t, l=l, r=r, s0=(i == 0): t.matmul(out_ap, l, r, start=s0, stop=True),
                                incs=[("s_t", 1)])
                    bankstate[bank]["prod"] = ("s_t", tok["s_t"])
                else:
                    pr.op("tensor",
                          lambda t, l=l, r=r, s0=(i == 0): t.matmul(out_ap, l, r, start=s0, stop=False))
            return bankstate[bank]["prod"]

        def evac(eng, bank, fn):
            p = bankstate[bank]["prod"]
            pr.wait(eng, p[0], p[1])
            s = "s_v" if eng == "vector" else "s_sc"
            tok = pr.op(eng, fn, incs=[(s, 1)])
            bankstate[bank]["cons"] = (s, tok[s])
            return (s, tok[s])

        wbuf_guard = {}

        def wdma(buf_key, fn):
            for g in wbuf_guard.get(buf_key, ()):
                pr.wait("gpsimd", g[0], g[1])
            pr.op("gpsimd", fn, incs=[("s_wd", 16)])
            return pr.now("s_wd")

        def wuse(buf_key, *toks):
            wbuf_guard[buf_key] = list(toks)

        def dump(name, src_ap, wait_sems):
            if name not in d_dbg:
                return
            for s, v in wait_sems:
                pr.wait("gpsimd", s, v)
            pr.op("gpsimd", lambda e, a=src_ap, o=d_dbg[name].ap(): e.dma_start(out=o, in_=a),
                  incs=[("s_out", 16)])

        def g_iota(out_ap, pattern, base, chan):
            pr.op("gpsimd",
                  lambda g, o=out_ap, p=pattern, b=base, c=chan:
                  g.iota(o, pattern=p, base=b, channel_multiplier=c,
                         allow_small_or_imprecise_dtypes=True),
                  incs=[("s_g", 1)])

        g_iota(pidx[:, :], [[0, 1]], 0, 1)
        g_iota(sl_full[:, :, :], [[0, 2], [1, 32]], 0, 0)
        g_iota(qd[:, :, :], [[128, 2], [0, 32]], 1, 1)
        g_iota(kd[:, :, :], [[-128, 2], [0, 32]], 255, -1)
        g_iota(delta_c[:, :, :], [[-128, 2], [1, 256]], 0, -1)
        g_iota(bdb[:, :, :], [[2, NPAIR], [0, 64]], 0, 0)
        g_iota(ident[:, :], [[-1, 128]], 0, 1)
        tok_g = pr.now("s_g")

        pr.op("vector", lambda v: v.memset(two_sb[:, :], 2.0))
        pr.op("vector", lambda v: v.memset(epsb[:, :], EPS))
        pr.op("vector", lambda v: v.memset(ones_bf[:, :], 1.0))
        pr.op("vector", lambda v: v.memset(state[:, :, :], 0.0))
        pr.wait("vector", "s_g", tok_g)
        pr.op("vector", lambda v: v.tensor_single_scalar(ident[:, :], ident[:, :], 0.0, ALU.is_equal))
        pr.op("vector", lambda v: v.tensor_single_scalar(phalf[:, :], pidx[:, :], 64.0, ALU.is_ge))
        pr.op("vector", lambda v: v.tensor_single_scalar(mask_c[:, :, :], delta_c[:, :, :], 0.0, ALU.is_ge),
              incs=[("s_v", 1)])
        tok_ones_v = pr.now("s_v")
        pr.wait("scalar", "s_v")
        pr.op("scalar", lambda s: s.activation(ln2[:, :], two_sb[:, :], AF.Ln), incs=[("s_sc", 1)])
        pr.wait("vector", "s_sc")
        pr.op("vector", lambda v: v.tensor_scalar(tmp_sl[:, :, :], sl_full[:, :, :], 1.0, -0.25, ALU.add, ALU.mult))
        pr.op("vector", lambda v: v.tensor_scalar(tmp_sl[:, :, :], tmp_sl[:, :, :], ln2[:, :], None, ALU.mult),
              incs=[("s_v", 1)])
        pr.wait("scalar", "s_v")
        pr.op("scalar", lambda s: s.activation(tmp_sl[:, :, :], tmp_sl[:, :, :], AF.Exp), incs=[("s_sc", 1)])
        pr.wait("vector", "s_sc")
        pr.op("vector", lambda v: v.scalar_tensor_tensor(sl_full[:, :, :], tmp_sl[:, :, :], 1e-5,
                                                         tmp_sl[:, :, :], ALU.mult, ALU.add),
              incs=[("s_v", 1)])
        pr.wait("vector", "s_v")
        pr.op("vector", lambda v: v.tensor_mul(qd[:, :, :], qd[:, :, :], sl_full[:, :, :]))
        pr.op("vector", lambda v: v.tensor_mul(kd[:, :, :], kd[:, :, :], sl_full[:, :, :]), incs=[("s_v", 1)])
        pr.wait("scalar", "s_v")
        pr.op("scalar", lambda s: s.activation(qd[:, :, :], qd[:, :, :], AF.Exp, scale=-1.0))
        pr.op("scalar", lambda s: s.activation(kd[:, :, :], kd[:, :, :], AF.Exp, scale=-1.0), incs=[("s_sc", 1)])
        pr.op("vector", lambda v: v.tensor_scalar(bdb[:, :, :], bdb[:, :, :], phalf[:, :], 1.0, ALU.add, ALU.add))
        pr.op("vector", lambda v: v.tensor_scalar_mul(bdb[:, :, :], bdb[:, :, :], -0.25))
        pr.op("vector", lambda v: v.tensor_scalar(bdb[:, :, :], bdb[:, :, :], ln2[:, :], None, ALU.mult),
              incs=[("s_v", 1)])
        pr.wait("scalar", "s_v")
        pr.op("scalar", lambda s: s.activation(bdb[:, :, :], bdb[:, :, :], AF.Exp), incs=[("s_sc", 1)])
        pr.wait("vector", "s_sc")
        pr.op("vector", lambda v: v.scalar_tensor_tensor(bdb[:, :, :], bdb[:, :, :], 1e-5,
                                                         bdb[:, :, :], ALU.mult, ALU.add), incs=[("s_v", 1)])
        pr.wait("vector", "s_v")
        pr.op("vector", lambda v: v.tensor_scalar_mul(bdb[:, :, :], bdb[:, :, :], -256.0), incs=[("s_v", 1)])
        pr.wait("scalar", "s_v")
        pr.op("scalar", lambda s: s.activation(bdb[:, :, :], bdb[:, :, :], AF.Exp), incs=[("s_sc", 1)])
        tok_tables_v = pr.now("s_v")
        tok_tables_sc = pr.now("s_sc")

        pr.op("sync", lambda e: e.dma_start(out=normw_sb[:, :],
                                            in_=d_normw.ap().rearrange("(k p) -> p k", p=128),
                                            allow_slow_non_contiguous=True),
              incs=[("s_xd", 16)])
        tok_normw = pr.now("s_xd")
        pr.op("gpsimd", lambda g: g.dma_start(out=rg_sb[:, :, :],
                                              in_=d_rgT.ap().rearrange("(k p) e -> p k e", p=128)),
              incs=[("s_wd", 16)])
        tok_rg = pr.now("s_wd")

        mbuf = st_f32[:, 0:512]
        msq = st_f32[:, 512:1024]
        m2b = st_f32[:, 1024:1536]
        t1b = st_f32[:, 1536:2048]

        stagbufs = [stagB, stagE]
        norm_tok = [0, 0]
        stats_tok = 0
        for grp in range(4):
            own = grp < 2
            tg = grp % 2
            src = d_x_fm if own else d_x_pfm
            dst = xn_f if own else xn_p
            sb_ = stagbufs[grp % 2]
            pr.wait("sync", "s_v", norm_tok[grp % 2])
            tok = pr.op("sync",
                        lambda e, s=src, tg=tg, sb_=sb_: e.dma_start(
                            out=sb_[:, :, :],
                            in_=s.ap().rearrange("(k p) t -> p k t", p=128)[:, :, tg * 512:(tg + 1) * 512]),
                        incs=[("s_xd", 16)])
            pr.wait("vector", "s_xd", tok["s_xd"])
            pr.wait("scalar", "s_xd", tok["s_xd"])
            pr.wait("vector", "s_t", stats_tok)
            pr.wait("scalar", "s_t", stats_tok)
            tok_cast = pr.op("vector", lambda v, sb_=sb_: v.tensor_copy(
                xbf.ap().rearrange("p a t -> p (a t)")[:, 0:NK * 512],
                sb_.ap().rearrange("p k t -> p (k t)")), incs=[("s_v", 1)])
            tok_sq = pr.op("scalar", lambda s, sb_=sb_: s.activation(
                xbf.ap().rearrange("p a t -> p (a t)")[:, NK * 512:2 * NK * 512],
                sb_.ap().rearrange("p k t -> p (k t)"), AF.Square), incs=[("s_sc", 1)])
            mm(0, pb[0][:, :], [(ones_bf[:, :], xbf[:, k, :]) for k in range(NK)],
               extra_wait=[("s_v", tok_cast["s_v"]), ("s_sc", tok_sq["s_sc"]), ("s_v", tok_ones_v)])
            stats_group = mm(1, pb[1][:, :], [(ones_bf[:, :], xbf[:, NK + k, :]) for k in range(NK)])
            stats_tok = stats_group[1]
            evac("vector", 0, lambda v: v.tensor_scalar_mul(mbuf, pb[0][:, :], 1.0 / D))
            evac("vector", 1, lambda v: v.tensor_scalar_mul(msq, pb[1][:, :], 1.0 / D))
            pr.op("vector", lambda v: v.tensor_mul(m2b, mbuf, mbuf))
            pr.op("vector", lambda v: v.tensor_sub(msq, msq, m2b), incs=[("s_v", 1)])
            pr.wait("scalar", "s_v")
            pr.op("scalar", lambda s: s.activation(m2b, msq, AF.Sqrt, bias=epsb[:, :]), incs=[("s_sc", 1)])
            pr.wait("vector", "s_sc")
            pr.op("vector", lambda v: v.reciprocal(msq, m2b))
            pr.op("vector", lambda v: v.tensor_mul(mbuf, mbuf, msq))
            for k in range(NK):
                pr.op("vector", lambda v, k=k, sb_=sb_: v.tensor_mul(t1b, sb_[:, k, :], msq))
                pr.op("vector",
                      lambda v, k=k, dst=dst, tg=tg: v.tensor_sub(
                          dst[:, k, tg * 512:(tg + 1) * 512], t1b, mbuf),
                      incs=[("s_v", 1)])
            norm_tok[grp % 2] = pr.now("s_v")
        tok_ln_all = pr.now("s_v")
        tok_xnp = tok_ln_all

        dump("xnf", xn_f.ap().rearrange("p k t -> p (k t)"), [("s_v", tok_ln_all)])
        dump("xnp", xn_p.ap().rearrange("p k t -> p (k t)"), [("s_v", tok_ln_all)])

        wuse(("pgrp", 0), ("s_v", tok_ln_all))
        wuse(("pgrp", 1), ("s_v", tok_ln_all))
        for pi, pname in enumerate(("k", "v")):
            for vg in range(4):
                gi = pi * 4 + vg
                par = gi % 2
                cols0 = D + pi * D + vg * 512
                dmatok = wdma(("pgrp", par),
                              lambda g, gi=gi, par=par: g.dma_start(
                                  out=pgrp[:, par, :, :],
                                  in_=d_pkv[gi].rearrange("p (k c) -> p k c", k=NK)))
                pr.wait("tensor", "s_wd", dmatok)
                pr.wait("tensor", "s_v", tok_ln_all)
                for tc in range(NT):
                    bank = tc % 4
                    prod = mm(bank, pb[bank][:, :],
                              [(xn_p[:, k, tc * 128:(tc + 1) * 128], pgrp[:, par, k, :]) for k in range(NK)])
                    if tc == NT - 1:
                        wuse(("pgrp", par), prod)
                    if pname == "v":
                        evac("scalar", bank,
                             lambda s, tc=tc, vg=vg, bank=bank: s.activation(
                                 v_p[:, tc, vg * 512:vg * 512 + 512], pb[bank][:, :], AF.Silu))
                    else:
                        pr.wait("scalar", "s_v", pr.now("s_v"))
                        evac("scalar", bank,
                             lambda s, bank=bank: s.activation(evtmp[:, :], pb[bank][:, :], AF.Silu))
                        pr.wait("vector", "s_sc")
                        pr.wait("vector", "s_v", tok_tables_v)
                        pr.wait("vector", "s_sc", tok_tables_sc)
                        for hh in range(8):
                            h = vg * 8 + hh
                            pr.op("vector",
                                  lambda v, tc=tc, vg=vg, hh=hh, h=h: v.tensor_scalar(
                                      kp_p[:, tc, vg * 512 + hh * 64: vg * 512 + hh * 64 + 64],
                                      evtmp[:, hh * 64:hh * 64 + 64],
                                      kd[:, tc % 2, h:h + 1], None, ALU.mult),
                                  incs=[("s_v", 1)] if hh == 7 else ())
        tok_kv_prev_sc = pr.now("s_sc")
        tok_kv_prev_v = pr.now("s_v")
        dump("kpp", kp_p.ap().rearrange("p a b -> p (a b)"), [("s_v", tok_kv_prev_v), ("s_sc", tok_kv_prev_sc)])
        dump("vpp", v_p.ap().rearrange("p a b -> p (a b)"), [("s_sc", tok_kv_prev_sc)])

        pr.wait("vector", "s_sc", tok_tables_sc)
        for blk in range(NBLK):
            for pair in range(NPAIR):
                bank = 4 + (pair % 2)
                mm(bank, pb[bank][:, 0:64],
                   [(kp_p[:, 2 * blk + cc, pair * 128:pair * 128 + 128],
                     v_p[:, 2 * blk + cc, (2 * pair) * 64:(2 * pair) * 64 + 64]) for cc in range(2)],
                   extra_wait=[("s_sc", tok_kv_prev_sc), ("s_v", tok_kv_prev_v)])
                prodB = mm(bank, pb[bank][:, 64:128],
                           [(kp_p[:, 2 * blk + cc, pair * 128:pair * 128 + 128],
                             v_p[:, 2 * blk + cc, (2 * pair + 1) * 64:(2 * pair + 1) * 64 + 64]) for cc in range(2)])
                pr.wait("vector", "s_t", prodB[1])
                pr.op("vector", lambda v, pair=pair: v.tensor_mul(
                    state[:, pair, :], state[:, pair, :], bdb[:, pair, :]))
                pr.op("vector", lambda v, pair=pair, bank=bank: v.tensor_add(
                    state[0:64, pair, :], state[0:64, pair, :], pb[bank][0:64, 0:64]))
                pr.op("vector", lambda v, pair=pair, bank=bank: v.tensor_add(
                    state[64:128, pair, :], state[64:128, pair, :], pb[bank][64:128, 64:128]))
                tok_f = pr.op("vector", lambda v, pair=pair: v.tensor_copy(
                    state_bf[:, pair, :], state[:, pair, :]), incs=[("s_v", 1)])
                bankstate[bank]["cons"] = ("s_v", tok_f["s_v"])
        tok_state = pr.now("s_v")
        tok_phaseA_t = pr.now("s_t")
        dump("state0", state.ap().rearrange("p a b -> p (a b)"), [("s_v", tok_state)])

        ddp = G.reshape([128, 40, 256])
        kpb = G.reshape([128, 10, 1024])

        wuse(("vgrp", 0), ("s_t", tok_phaseA_t))
        wuse(("vgrp", 1), ("s_t", tok_phaseA_t))
        tok_v_sc = [0, 0, 0, 0]
        for vg in range(4):
            par = vg % 2
            dmatok = wdma(("vgrp", par),
                          lambda g, vg=vg, par=par: g.dma_start(
                              out=vgrp[:, par, :, :],
                              in_=d_vg[vg].rearrange("p (k c) -> p k c", k=NK)))
            pr.wait("tensor", "s_wd", dmatok)
            pr.wait("tensor", "s_v", tok_ln_all)
            for tc in range(NT):
                bank = tc % 2
                prod = mm(bank, pb[bank][:, :],
                          [(xn_f[:, k, tc * 128:(tc + 1) * 128], vgrp[:, par, k, :]) for k in range(NK)])
                if tc == NT - 1:
                    wuse(("vgrp", par), prod)
                evac("scalar", bank,
                     lambda s, tc=tc, vg=vg, bank=bank: s.activation(
                         v_tm[:, tc, vg * 512:vg * 512 + 512], pb[bank][:, :], AF.Silu))
            tok_v_sc[vg] = pr.now("s_sc")
        tok_qkv_own_sc = pr.now("s_sc")

        def gen_dd_batch(b):
            pr.wait("vector", "s_v", tok_tables_v)
            pr.wait("scalar", "s_sc", tok_tables_sc)
            par16 = (b % 2) * 16
            for j in range(8):
                loc, hb = j // 2, j % 2
                h = 2 * (4 * b + loc) + hb
                pr.wait("vector", "s_sc", gen_exp_tok[0])
                pr.wait("vector", "s_v", gen_mul_tok[0])
                for jc in range(2):
                    t1 = pr.op("vector", lambda v, jc=jc, h=h: v.tensor_scalar(
                        evtmp[:, jc * 256:jc * 256 + 256], delta_c[:, jc, :],
                        sl_full[:, 0, h:h + 1], 0.0, ALU.mult, ALU.max),
                        incs=[("s_v", 1)] if jc == 1 else ())
                pr.wait("scalar", "s_v", t1["s_v"])
                t2 = pr.op("scalar", lambda s: s.activation(evtmp[:, 0:512], evtmp[:, 0:512],
                                                            AF.Exp, scale=-1.0),
                           incs=[("s_sc", 1)])
                gen_exp_tok[0] = t2["s_sc"]
                pr.wait("vector", "s_sc", t2["s_sc"])
                for jc in range(2):
                    t3 = pr.op("vector", lambda v, par16=par16, loc=loc, hb=hb, jc=jc: v.tensor_mul(
                        ddp[:, par16 + loc * 4 + hb * 2 + jc, :],
                        evtmp[:, jc * 256:jc * 256 + 256], mask_c[:, jc, :]),
                        incs=[("s_v", 1)] if jc == 1 else ())
                gen_mul_tok[0] = t3["s_v"]
            return pr.now("s_v")

        gen_exp_tok = [0, 0]
        gen_mul_tok = [0, 0]
        tok_fold_kind = [("s_v", tok_state)] * NPAIR
        tok_cast_last = 0
        evtmp_guard = [pr.now("s_v"), pr.now("s_v")]
        gen_dd_batch(0)

        tok_kq = {}
        tok_kq = {}
        for pair in range(NPAIR):
            pp_ = pair % 2
            for wi in range(2):
                par = (pair * 2 + wi) % 2
                gidx = pair if wi == 0 else 16 + pair
                dmatok = wdma(("wq", par),
                              lambda g, gidx=gidx, par=par: g.dma_start(
                                  out=wq[:, par, :, :],
                                  in_=d_qkg[gidx].rearrange("p (k c) -> p k c", k=NK)))
                pr.wait("tensor", "s_wd", dmatok)
                dst = k_f if wi == 0 else q_f
                for tg in range(2):
                    bank = tg
                    prod = mm(bank, pb[bank][:, :],
                              [(wq[:, par, k, :], xn_f[:, k, tg * 512:(tg + 1) * 512]) for k in range(NK)])
                    if tg == 1:
                        wuse(("wq", par), prod)
                    evac("scalar", bank,
                         lambda s, dst=dst, pair=pair, tg=tg, bank=bank: s.activation(
                             dst[:, pair, tg * 512:(tg + 1) * 512], pb[bank][:, :], AF.Silu))
            tok_kq[pair] = pr.now("s_sc")
            if pair % 4 == 0 and pair + 4 <= NPAIR - 1:
                gen_dd_batch(pair // 4 + 1)

            pr.wait("tensor", "s_sc", tok_kq[pair])
            pr.wait("tensor", "s_sc", tok_v_sc[pair // 4])
            bawA = 4
            bawB = 5
            bii = 6 if pp_ == 0 else 3
            for blk in range(NBLK):
                bp = blk % 2
                c = bankstate[7]["cons"]
                if c:
                    pr.wait("tensor", c[0], c[1])
                for tc2 in range(2):
                    tok_tr = pr.op("tensor", lambda t, pair=pair, blk=blk, tc2=tc2: t.transpose(
                        pt[:, tc2 * 128:tc2 * 128 + 128],
                        k_f[:, pair, blk * 256 + tc2 * 128: blk * 256 + tc2 * 128 + 128],
                        ident[:, :]), incs=[("s_t", 1)] if tc2 == 1 else ())
                bankstate[7]["prod"] = ("s_t", pr.now("s_t"))
                pr.wait("scalar", "s_t", pr.now("s_t"))
                for tc2 in range(2):
                    for hb in range(2):
                        h = 2 * pair + hb
                        tok_e = pr.op("scalar", lambda s, bp=bp, tc2=tc2, hb=hb, h=h: s.activation(
                            kpb[:, 8 + bp, tc2 * 512 + hb * 64: tc2 * 512 + hb * 64 + 64],
                            pt[:, tc2 * 128 + hb * 64: tc2 * 128 + hb * 64 + 64],
                            AF.Copy, scale=kd[:, tc2, h:h + 1]),
                            incs=[("s_sc", 1)] if tc2 == 1 and hb == 1 else ())
                bankstate[7]["cons"] = ("s_sc", tok_e["s_sc"])
                tok_kp = ("s_sc", pr.now("s_sc"))
                for jc in range(2):
                    mm(bawA, pb[bawA][:, jc * 256:(jc + 1) * 256],
                       [(k_f[0:64, pair, blk * 256 + jc * 128: blk * 256 + jc * 128 + 128],
                         q_f[0:64, pair, blk * 256: blk * 256 + 256])])
                    mm(bawB, pb[bawB][:, jc * 256:(jc + 1) * 256],
                       [(k_f[64:128, pair, blk * 256 + jc * 128: blk * 256 + jc * 128 + 128],
                         q_f[64:128, pair, blk * 256: blk * 256 + 256])])
                for hb in range(2):
                    h = 2 * pair + hb
                    baw = bawA if hb == 0 else bawB
                    if hb == 1:
                        pr.wait("vector", "s_t", pr.now("s_t"))
                    dsl = (pair // 4 % 2) * 16 + (pair % 4) * 4 + hb * 2
                    evac("vector", baw,
                         lambda v, baw=baw, dsl=dsl, pp_=pp_: v.tensor_mul(
                             awd[:, pp_, :, :], pb[baw][:, 0:512].rearrange("p (j c) -> p j c", j=2),
                             ddp[:, dsl:dsl + 2, :]))
                    tok_awd = pr.now("s_v")
                    for ic in range(2):
                        off = hb * 256 + ic * 128
                        mm(bii, pb[bii][:, off:off + 64],
                           [(awd[:, pp_, jc, ic * 128:ic * 128 + 128],
                             v_tm[:, 2 * blk + jc, h * 64:h * 64 + 64]) for jc in range(2)],
                           extra_wait=[("s_v", tok_awd)])
                        pm = 0 if hb == 0 else 64
                        mm(bii, pb[bii][:, off + 64:off + 128],
                           [(q_f[pm:pm + 64, pair, blk * 256 + ic * 128: blk * 256 + ic * 128 + 128],
                             state_bf[pm:pm + 64, pair, :])],
                           extra_wait=[tok_fold_kind[pair]])
                prod = bankstate[bii]["prod"]
                pr.wait("vector", prod[0], prod[1])
                pr.wait("vector", "s_v", evtmp_guard[pp_])
                for hb in range(2):
                    for ic in range(2):
                        off = hb * 256 + ic * 128
                        eoff = pp_ * 256 + hb * 128 + ic * 64
                        pr.op("vector", lambda v, bii=bii, off=off, eoff=eoff, ic=ic, hb=hb, pair=pair: v.tensor_scalar(
                            evtmp[:, eoff:eoff + 64], pb[bii][:, off + 64:off + 128],
                            qd[:, ic, 2 * pair + hb: 2 * pair + hb + 1], None, ALU.mult))
                out4 = attn_sb.ap()[:, 2 * blk:2 * blk + 2, 2 * pair * 64:(2 * pair + 2) * 64] \
                    .rearrange("p i (hb c) -> p hb i c", hb=2)
                in04 = evtmp[:, pp_ * 256:pp_ * 256 + 256].rearrange("p (hb i c) -> p hb i c", hb=2, i=2)
                in14 = pb[bii][:, 0:512].rearrange("p (hb i c2) -> p hb i c2", hb=2, i=2)[:, :, :, 0:64]
                tok_a = pr.op("vector", lambda v, out4=out4, in04=in04, in14=in14: v.tensor_add(
                    out4, in04, in14), incs=[("s_v", 1)])
                bankstate[bii]["cons"] = ("s_v", tok_a["s_v"])
                evtmp_guard[pp_] = tok_a["s_v"]
                mm(2, pb[2][:, 0:64],
                   [(kpb[:, 8 + bp, cc * 512:cc * 512 + 128],
                     v_tm[:, 2 * blk + cc, (2 * pair) * 64:(2 * pair) * 64 + 64]) for cc in range(2)],
                   extra_wait=[tok_kp])
                prodB = mm(2, pb[2][:, 64:128],
                           [(kpb[:, 8 + bp, cc * 512:cc * 512 + 128],
                             v_tm[:, 2 * blk + cc, (2 * pair + 1) * 64:(2 * pair + 1) * 64 + 64]) for cc in range(2)])
                pr.wait("vector", "s_t", prodB[1])
                pr.wait("vector", "s_sc", tok_cast_last)
                pr.op("vector", lambda v, pair=pair: v.tensor_mul(
                    state[:, pair, :], state[:, pair, :], bdb[:, pair, :]))
                pr.op("vector", lambda v, pair=pair: v.tensor_add(
                    state[0:64, pair, :], state[0:64, pair, :], pb[2][0:64, 0:64]))
                tok_fv = pr.op("vector", lambda v, pair=pair: v.tensor_add(
                    state[64:128, pair, :], state[64:128, pair, :], pb[2][64:128, 64:128]),
                    incs=[("s_v", 1)])
                pr.wait("scalar", "s_v", tok_fv["s_v"])
                tok_f = pr.op("scalar", lambda s, pair=pair: s.copy(
                    state_bf[:, pair, :], state[:, pair, :]), incs=[("s_sc", 1)])
                bankstate[2]["cons"] = ("s_v", tok_fv["s_v"])
                tok_cast_last = tok_f["s_sc"]
                tok_fold_kind[pair] = ("s_sc", tok_f["s_sc"])
        tok_attn = pr.now("s_v")
        tok_scan_t = pr.now("s_t")
        dump("attn", attn_sb.ap().rearrange("p a b -> p (a b)"), [("s_v", tok_attn)])

        pr.wait("scalar", "s_v", tok_attn)
        for tc in range(NT):
            pr.op("scalar", lambda s, tc=tc: s.activation(
                sqtmp[:, 0:2048], attn_sb[:, tc, :], AF.Square, accum_out=rms_acc[:, tc:tc + 1]),
                incs=[("s_sc", 1)] if tc == NT - 1 else ())
        pr.wait("vector", "s_sc")
        pr.op("vector", lambda v: v.tensor_scalar_mul(rms_acc[:, :], rms_acc[:, :], 1.0 / D), incs=[("s_v", 1)])
        pr.wait("scalar", "s_v")
        pr.op("scalar", lambda s: s.activation(rstd_a[:, :], rms_acc[:, :], AF.Sqrt, bias=epsb[:, :]), incs=[("s_sc", 1)])
        pr.wait("vector", "s_sc")
        pr.op("vector", lambda v: v.reciprocal(rstd_a[:, :], rstd_a[:, :]), incs=[("s_v", 1)])
        tok_rstd = pr.now("s_v")

        pr.wait("tensor", "s_wd", tok_rg)
        wuse(("ggrp", 0), ("s_t", tok_scan_t))
        wuse(("ggrp", 1), ("s_t", tok_scan_t))
        for gg in range(4):
            par = gg % 2
            dmatok = wdma(("ggrp", par),
                          lambda g, gg=gg, par=par: g.dma_start(
                              out=ggrp[:, par, :, :],
                              in_=d_gateg[gg].rearrange("p (k c) -> p k c", k=NK)))
            pr.wait("tensor", "s_wd", dmatok)
            for tc in range(NT):
                bank = tc % 2
                bank_acquire(bank)
                if gg == 0:
                    bank_acquire(4)
                for k in range(NK):
                    isl = k == NK - 1
                    tokg = pr.op("tensor", lambda t, k=k, tc=tc, par=par, bank=bank: t.matmul(
                        pb[bank][:, :], xn_f[:, k, tc * 128:(tc + 1) * 128], ggrp[:, par, k, :],
                        start=(k == 0), stop=isl), incs=[("s_t", 1)] if isl else ())
                    if gg == 0:
                        tokr = pr.op("tensor", lambda t, k=k, tc=tc: t.matmul(
                            pb[4][:, tc * 8:tc * 8 + 8], xn_f[:, k, tc * 128:(tc + 1) * 128],
                            rg_sb[:, k, :], start=(k == 0), stop=isl),
                            incs=[("s_t", 1)] if isl else ())
                bankstate[bank]["prod"] = ("s_t", tokg["s_t"])
                if gg == 0:
                    bankstate[4]["prod"] = ("s_t", tokr["s_t"])
                if tc == NT - 1:
                    wuse(("ggrp", par), bankstate[bank]["prod"])
                evac("scalar", bank, lambda s, tc=tc, gg=gg, bank=bank: s.activation(
                    gate_sb[:, tc, gg * 512:(gg + 1) * 512], pb[bank][:, :], AF.Sigmoid))
        tok_gate_sc = pr.now("s_sc")

        prod4 = bankstate[4]["prod"]
        pr.wait("vector", prod4[0], prod4[1])
        def vb(fn):
            pr.op("vector", fn, incs=[("s_v", 1)])
            pr.wait("vector", "s_v")
        for tc in range(NT):
            zs = pb[4][:, tc * 8:tc * 8 + 8]
            vb(lambda v, zs=zs: v.reduce_max(rr[:, 0:1], zs, AX.X))
            vb(lambda v, zs=zs: v.tensor_single_scalar(rr[:, 16:24], zs, rr[:, 0:1], ALU.is_equal))
            vb(lambda v, zs=zs: v.scalar_tensor_tensor(rr[:, 8:16], rr[:, 16:24], -1e30, zs,
                                                       ALU.mult, ALU.add))
            vb(lambda v: v.reduce_max(rr[:, 1:2], rr[:, 8:16], AX.X))
            vb(lambda v, zs=zs: v.tensor_single_scalar(rr[:, 16:24], zs, rr[:, 1:2], ALU.is_ge))
            vb(lambda v: v.tensor_scalar_mul(rr[:, 0:1], rr[:, 0:1], -1.0))
            pr.wait("scalar", "s_v")
            pr.op("scalar", lambda s, zs=zs: s.activation(rr[:, 2:10], zs, AF.Exp, bias=rr[:, 0:1]),
                  incs=[("s_sc", 1)])
            pr.wait("vector", "s_sc")
            vb(lambda v: v.tensor_mul(rr[:, 2:10], rr[:, 2:10], rr[:, 16:24]))
            vb(lambda v: v.reduce_sum(rr[:, 1:2], rr[:, 2:10], AX.X))
            vb(lambda v: v.reciprocal(rr[:, 1:2], rr[:, 1:2]))
            tok_cw = pr.op("vector", lambda v, tc=tc: v.tensor_scalar(
                cw[:, tc, :], rr[:, 2:10], rr[:, 1:2], None, ALU.mult), incs=[("s_v", 1)])
        bankstate[4]["cons"] = ("s_v", tok_cw["s_v"])
        tok_cw_all = pr.now("s_v")
        dump("cw", cw.ap().rearrange("p a b -> p (a b)"), [("s_v", tok_cw_all)])

        s1tmp = st_f32[:, 1024:3072]
        pr.wait("vector", "s_sc", tok_gate_sc)
        pr.wait("vector", "s_v", tok_rstd)
        for tc in range(NT):
            pr.op("vector", lambda v, tc=tc: v.tensor_scalar(
                s1tmp, attn_sb[:, tc, :], rstd_a[:, tc:tc + 1], None, ALU.mult))
            pr.op("vector", lambda v, tc=tc: v.tensor_mul(attn_sb[:, tc, :], s1tmp, gate_sb[:, tc, :]),
                  incs=[("s_v", 1)])
        tok_spre_tm = pr.now("s_v")

        pr.wait("tensor", "s_v", tok_spre_tm)
        pr.wait("vector", "s_xd", tok_normw)
        for bt in range(NT * NK // 8):
            c = bankstate[7]["cons"]
            if c:
                pr.wait("tensor", c[0], c[1])
            for i8 in range(8):
                idx = bt * 8 + i8
                tc, k = idx // NK, idx % NK
                pr.op("tensor", lambda t, tc=tc, k=k, i8=i8: t.transpose(
                    pt[:, i8 * 128:i8 * 128 + 128],
                    attn_sb[:, tc, k * 128:(k + 1) * 128], ident[:, :]),
                    incs=[("s_t", 1)] if i8 == 7 else ())
            bankstate[7]["prod"] = ("s_t", pr.now("s_t"))
            pr.wait("vector", "s_t", pr.now("s_t"))
            for i8 in range(8):
                idx = bt * 8 + i8
                tc, k = idx // NK, idx % NK
                tok_e = pr.op("vector", lambda v, tc=tc, k=k, i8=i8: v.tensor_scalar(
                    s_pre_f[:, k, tc * 128:(tc + 1) * 128], pt[:, i8 * 128:i8 * 128 + 128],
                    normw_sb[:, k:k + 1], None, ALU.mult),
                    incs=[("s_v", 1)] if i8 == 7 else ())
            bankstate[7]["cons"] = ("s_v", tok_e["s_v"])
        tok_spref = pr.now("s_v")
        dump("spref", s_pre_f.ap().rearrange("p k t -> p (k t)"), [("s_v", tok_spref)])

        atile_guard = None
        for m in range(NK):
            for wi, wsrc in ((0, d_sgg), (1, d_sug)):
                par = (m * 2 + wi) % 2
                dmatok = wdma(("wq", par),
                              lambda g, wsrc=wsrc, m=m, par=par: g.dma_start(
                                  out=wq[:, par, :, :],
                                  in_=wsrc[m].rearrange("p (k c) -> p k c", k=NK)))
                pr.wait("tensor", "s_wd", dmatok)
                for tg in range(2):
                    bank = (m * 4 + wi * 2 + tg) % 4
                    prod = mm(bank, pb[bank][:, :],
                              [(wq[:, par, k, :], xn_f[:, k, tg * 512:(tg + 1) * 512]) for k in range(NK)])
                    if tg == 1:
                        wuse(("wq", par), prod)
                    if wi == 0:
                        if atile_guard:
                            pr.wait("scalar", atile_guard[0], atile_guard[1])
                        evac("scalar", bank, lambda s, tg=tg, bank=bank: s.activation(
                            atile[:, tg * 512:(tg + 1) * 512], pb[bank][:, :], AF.Silu))
                    else:
                        pr.wait("vector", "s_sc")
                        tok_h = evac("vector", bank, lambda v, m=m, tg=tg, bank=bank: v.tensor_mul(
                            h_f[:, m, tg * 512:(tg + 1) * 512], pb[bank][:, :],
                            atile[:, tg * 512:(tg + 1) * 512]))
                        if tg == 1:
                            atile_guard = tok_h
        tok_hf = pr.now("s_v")
        wuse(("wgrpG", 0), ("s_v", tok_spre_tm), ("s_sc", tok_gate_sc), ("s_t", tok_scan_t))
        wuse(("wgrpG", 1), ("s_v", tok_spre_tm), ("s_sc", tok_gate_sc), ("s_t", tok_scan_t))
        for og in range(8):
            par = og % 2
            dmatok = wdma(("wgrpG", par),
                          lambda g, og=og, par=par: g.dma_start(
                              out=wgrpG[:, par, 0:16, :],
                              in_=d_sdg[og].rearrange("p (k c) -> p k c", k=NK)))
            pr.wait("tensor", "s_wd", dmatok)
            pr.wait("tensor", "s_v", tok_hf)
            for tc in range(NT):
                bank = tc % 4
                prod = mm(bank, pb[bank][:, 0:256],
                          [(h_f[:, k, tc * 128:(tc + 1) * 128], wgrpG[:, par, k, :]) for k in range(NK)])
                if tc == NT - 1:
                    wuse(("wgrpG", par), prod)
                evac("vector", bank, lambda v, tc=tc, og=og, bank=bank: v.tensor_copy(
                    y_sl(tc, og * 256, og * 256 + 256), pb[bank][:, 0:256]))
        tok_y_shared = pr.now("s_v")

        for e in range(IEXP):
            for m in range(8):
                for wi, wsrc in ((0, d_wgg), (1, d_wug)):
                    par = (m * 2 + wi) % 2
                    dmatok = wdma(("wq", par),
                                  lambda g, wsrc=wsrc, e=e, m=m, par=par: g.dma_start(
                                      out=wq[:, par, :, :],
                                      in_=wsrc[e, m].rearrange("p (k c) -> p k c", k=NK)))
                    pr.wait("tensor", "s_wd", dmatok)
                    for tg in range(2):
                        bank = (m * 4 + wi * 2 + tg) % 4
                        prod = mm(bank, pb[bank][:, :],
                                  [(wq[:, par, k, :], xn_f[:, k, tg * 512:(tg + 1) * 512]) for k in range(NK)])
                        if tg == 1:
                            wuse(("wq", par), prod)
                        if wi == 0:
                            if atile_guard:
                                pr.wait("scalar", atile_guard[0], atile_guard[1])
                            evac("scalar", bank, lambda s, tg=tg, bank=bank: s.activation(
                                atile[:, tg * 512:(tg + 1) * 512], pb[bank][:, :], AF.Silu))
                        else:
                            pr.wait("vector", "s_sc")
                            tok_h = evac("vector", bank, lambda v, m=m, tg=tg, bank=bank: v.tensor_mul(
                                h_e[:, m, tg * 512:(tg + 1) * 512], pb[bank][:, :],
                                atile[:, tg * 512:(tg + 1) * 512]))
                            if tg == 1:
                                atile_guard = tok_h
            tok_he = pr.now("s_v")
            for og in range(8):
                par = og % 2
                dmatok = wdma(("wgrpG", par),
                              lambda g, e=e, og=og, par=par: g.dma_start(
                                  out=wgrpG[:, par, 0:8, :],
                                  in_=d_wdg[e, og].rearrange("p (k c) -> p k c", k=8)))
                pr.wait("tensor", "s_wd", dmatok)
                pr.wait("tensor", "s_v", tok_he)
                for tc in range(NT):
                    bank = tc % 4
                    prod = mm(bank, pb[bank][:, 0:256],
                              [(h_e[:, k, tc * 128:(tc + 1) * 128], wgrpG[:, par, k, :]) for k in range(8)])
                    if tc == NT - 1:
                        wuse(("wgrpG", par), prod)
                    pr.wait("vector", "s_v", tok_cw_all)
                    pr.wait("vector", "s_v", tok_y_shared)
                    evac("vector", bank, lambda v, tc=tc, og=og, e=e, bank=bank: v.scalar_tensor_tensor(
                        y_sl(tc, og * 256, og * 256 + 256), pb[bank][:, 0:256], cw[:, tc, e:e + 1],
                        y_sl(tc, og * 256, og * 256 + 256), ALU.mult, ALU.add))
        tok_y = pr.now("s_v")
        dump("y", y_lo.ap().rearrange("p a b -> p (a b)"), [("s_v", tok_y)])

        pr.wait("scalar", "s_v", tok_y)
        tok_g_tc = [0] * NT
        for tc in list(range(4, NT)) + list(range(4)):
            t_ = pr.op("scalar", lambda s, tc=tc: s.activation(
                g_sb[:, tc, :], (y_lo if tc < 4 else y_hi)[:, tc % 4, :], AF.Silu),
                incs=[("s_sc", 1)])
            tok_g_tc[tc] = t_["s_sc"]
            if tc == NT - 1:
                tok_yhi_sc = t_["s_sc"]
        tok_g_sc = pr.now("s_sc")
        dump("g", g_sb.ap().rearrange("p a b -> p (a b)"), [("s_sc", tok_g_sc)])

        n_out = 0
        sout_base = pr.now("s_out")
        for meta in range(2):
            pr.wait("sync", "s_sc", tok_yhi_sc)
            tok_x = pr.op("sync", lambda s, meta=meta: s.dma_start(
                out=xstage[:, :, :],
                in_=d_x_tm.ap().rearrange("(a p) c -> p a c", p=128)[:, :, meta * 1024:(meta + 1) * 1024]),
                incs=[("s_xd", 16)])
            for ogl in range(4):
                og = meta * 4 + ogl
                par = og % 2
                dmatok = wdma(("wgrpG", par),
                              lambda g, og=og, par=par: g.dma_start(
                                  out=wgrpG[:, par, 0:16, :],
                                  in_=d_outg[og].rearrange("p (k c) -> p k c", k=NK)))
                pr.wait("tensor", "s_wd", dmatok)
                pr.wait("tensor", "s_v", tok_spref)
                for tc in range(NT):
                    bank = tc % 4
                    prod = mm(bank, pb[bank][:, 0:256],
                              [(s_pre_f[:, k, tc * 128:(tc + 1) * 128], wgrpG[:, par, k, :]) for k in range(NK)])
                    if tc == NT - 1:
                        wuse(("wgrpG", par), prod)
                    pr.wait("vector", "s_xd", tok_x["s_xd"])
                    pr.wait("vector", "s_sc", tok_g_tc[tc])
                    obuf = n_out % 2
                    pr.wait("vector", "s_out", sout_base + max(0, (n_out - 1) * 16))
                    evac("vector", bank, lambda v, tc=tc, og=og, bank=bank: v.tensor_mul(
                        y_lo[:, 0, 0:256], pb[bank][:, 0:256], g_sb[:, tc, og * 256:(og + 1) * 256]))
                    tok_o = pr.op("vector", lambda v, tc=tc, ogl=ogl, obuf=obuf: v.tensor_add(
                        out_sbv[:, obuf, :], y_lo[:, 0, 0:256],
                        xstage[:, tc, ogl * 256:(ogl + 1) * 256]), incs=[("s_v", 1)])
                    pr.wait("sync", "s_v", tok_o["s_v"])
                    pr.op("sync", lambda s, tc=tc, og=og, obuf=obuf: s.dma_start(
                        out=d_out.ap()[tc * 128:(tc + 1) * 128, og * 256:(og + 1) * 256],
                        in_=out_sbv[:, obuf, :]), incs=[("s_out", 16)])
                    n_out += 1
        pr.wait("sync", "s_out", pr.now("s_out"))

        def run_engine(eng_name):
            def f(eng):
                for item in pr.items[eng_name]:
                    if item[0] == "w":
                        eng.wait_ge(sems[item[1]], item[2])
                    else:
                        inst = item[1](eng)
                        for s, a in item[2]:
                            inst.then_inc(sems[s], a)
            return f

        block.tensor(run_engine("tensor"))
        block.vector(run_engine("vector"))
        block.scalar(run_engine("scalar"))
        block.gpsimd(run_engine("gpsimd"))
        block.sync(run_engine("sync"))

    return nc


def _prep_inputs(L0_attn_hidden_states, L0_attn_qkv_w, L0_attn_out_w, L0_attn_gate_w,
                 L0_attn_norm_w, L0_ffn_gate_w, L0_ffn_wg, L0_ffn_wu, L0_ffn_wd,
                 L0_ffn_sg, L0_ffn_su, L0_ffn_sd):
    x = np.asarray(L0_attn_hidden_states)
    qkv = np.asarray(L0_attn_qkv_w)
    qkv_re = qkv.reshape(32, 3, 64, 2048).transpose(1, 0, 2, 3).reshape(6144, 2048)
    qkvT = np.ascontiguousarray(qkv_re.T)

    def grp(W, w):
        C = W.shape[1]
        return np.ascontiguousarray(
            W.reshape(16, 128, C).transpose(1, 0, 2)
             .reshape(128, 16, C // w, w).transpose(2, 0, 1, 3)
             .reshape(C // w, 128, 16 * w))

    gateT = np.asarray(L0_attn_gate_w).T
    outT = np.asarray(L0_attn_out_w).T
    sgw = np.asarray(L0_ffn_sg); suw = np.asarray(L0_ffn_su); sdw = np.asarray(L0_ffn_sd)
    wgw = np.asarray(L0_ffn_wg); wuw = np.asarray(L0_ffn_wu); wdw = np.asarray(L0_ffn_wd)
    qk_all = grp(qkvT[:, 0:4096], 128)
    qkg = np.concatenate([qk_all[16:32], qk_all[0:16]], axis=0)
    shared = dict(
        qkg=np.ascontiguousarray(qkg),
        pkv=grp(qkvT[:, 2048:6144], 512),
        vg=grp(qkvT[:, 4096:6144], 512),
        gateg=grp(gateT, 512),
        outg=grp(outT, 256),
        rgT=np.ascontiguousarray(np.asarray(L0_ffn_gate_w).T),
        wgg=np.stack([grp(wgw[e], 128) for e in range(8)]),
        wug=np.stack([grp(wuw[e], 128) for e in range(8)]),
        wdg=np.stack([np.ascontiguousarray(
            sdwe.reshape(8, 128, 2048).transpose(1, 0, 2)
                .reshape(128, 8, 8, 256).transpose(2, 0, 1, 3).reshape(8, 128, 2048))
            for sdwe in wdw]),
        sgg=grp(sgw, 128),
        sug=grp(suw, 128),
        sdg=grp(sdw, 256),
        normw=np.ascontiguousarray(L0_attn_norm_w),
    )
    zeros_fm = np.zeros((D, TOK), np.float32)
    in_maps = []
    for c in range(8):
        b, h = c // 2, c % 2
        own = x[b, h * TOK:(h + 1) * TOK, :]
        m = dict(shared)
        m["x_fm"] = np.ascontiguousarray(own.T)
        m["x_tm"] = np.ascontiguousarray(own)
        m["x_pfm"] = np.ascontiguousarray(x[b, 0:TOK, :].T) if h == 1 else zeros_fm
        in_maps.append(m)
    return in_maps


def kernel(**inputs):
    in_maps = _prep_inputs(**inputs)
    nc = build()
    res = run_bass_kernel_spmd(nc, in_maps, core_ids=list(range(8)))
    out = np.empty((4, 2048, 2048), np.float32)
    for c in range(8):
        b, h = c // 2, c % 2
        out[b, h * TOK:(h + 1) * TOK, :] = res.results[c]["out"]
    return out
